# version 49
# baseline (speedup 1.0000x reference)
import numpy as np
import concourse.bass as bass
import concourse.mybir as mybir
from concourse.bass_utils import run_bass_kernel_spmd

F32 = mybir.dt.float32
BF16 = mybir.dt.bfloat16
AF = mybir.ActivationFunctionType
ALU = mybir.AluOpType
AX = mybir.AxisListType

S, D, G, F = 4096, 1024, 512, 4096
NCORE = 8



class Buf:

    __slots__ = ("wtok", "rtoks")

    def __init__(self):
        self.wtok = None
        self.rtoks = []


class Em:
    ENGS = ("sync", "gpsimd", "scalar", "vector", "tensor")

    def __init__(self, nc):
        self.nc = nc
        self.q = {e: [] for e in self.ENGS}
        self.sem = {e: nc.alloc_semaphore(f"prog_{e}") for e in self.ENGS}
        self.cnt = {e: 0 for e in self.ENGS}
        self.dsem = {}
        self.dcnt = {}
        self.wm = {e: {} for e in self.ENGS}

    def _collect(self, reads, writes, extra):
        deps = [b.wtok for b in reads]
        for b in writes:
            deps.extend(b.rtoks)
            deps.append(b.wtok)
        deps.extend(extra)
        out = []
        for t in deps:
            if isinstance(t, list):
                out.extend(t)
            else:
                out.append(t)
        return out

    def _waits(self, eng, deps):
        waits = []
        best = {}
        for tok in deps:
            if tok is None:
                continue
            kind, src, val = tok
            key = (kind, src)
            if val > best.get(key, 0):
                best[key] = val
        for key, val in best.items():
            kind, src = key
            if val > self.wm[eng].get(key, 0):
                sem = self.sem[src] if kind == "p" else self.dsem[src]
                waits.append((sem, val if kind == "p" else 16 * val))
                self.wm[eng][key] = val
        return waits

    def _record(self, tok, reads, writes):
        for b in reads:
            b.rtoks.append(tok)
        for b in writes:
            b.wtok = tok
            b.rtoks = []

    def op(self, eng, fn, reads=(), writes=(), extra=()):
        waits = self._waits(eng, self._collect(reads, writes, extra))
        self.cnt[eng] += 1
        tok = ("p", eng, self.cnt[eng])
        self.q[eng].append((waits, fn, (self.sem[eng], 1)))
        self._record(tok, reads, writes)
        return tok

    def group(self, eng, fns, reads=(), writes=(), extra=()):
        waits = self._waits(eng, self._collect(reads, writes, extra))
        self.cnt[eng] += 1
        tok = ("p", eng, self.cnt[eng])
        n = len(fns)
        for i, fn in enumerate(fns):
            self.q[eng].append((waits if i == 0 else [], fn,
                               (self.sem[eng], 1) if i == n - 1 else None))
        self._record(tok, reads, writes)
        return tok

    def dma(self, eng, chan, out, in_, reads=(), writes=(), extra=(),
            chain=False):
        key = (eng, chan)
        if key not in self.dsem:
            self.dsem[key] = self.nc.alloc_semaphore(f"dma_{eng}_{chan}")
            self.dcnt[key] = 0
        if chain and self.dcnt[key] > 0:
            extra = list(extra) + [("d", key, self.dcnt[key])]
        waits = self._waits(eng, self._collect(reads, writes, extra))
        self.dcnt[key] += 1
        tok = ("d", key, self.dcnt[key])

        def fn(e, out=out, in_=in_):
            return e.dma_start(out=out, in_=in_)

        self.q[eng].append((waits, fn, (self.dsem[key], 16)))
        self._record(tok, reads, writes)
        return tok

    def dma_total(self, eng, chan):
        key = (eng, chan)
        return ("d", key, self.dcnt.get(key, 0))

    def replay(self):
        nc = self.nc
        with nc.Block() as block:
            for ename in self.ENGS:
                ops = self.q[ename]

                def body(eng, ops=ops):
                    for waits, fn, inc in ops:
                        for sem, val in waits:
                            eng.wait_ge(sem, val)
                        inst = fn(eng)
                        if inc is not None:
                            inst.then_inc(*inc)

                getattr(block, ename)(body)




def build(nch=32):
    T = nch * 128
    nc = bass.Bass("TRN2", target_bir_lowering=False)

    x_hbm = nc.dram_tensor("x", [T, D], F32, kind="ExternalInput").ap()
    out_hbm = nc.dram_tensor("out", [T, D], F32, kind="ExternalOutput").ap()
    guid_hbm = nc.dram_tensor("guid", [G, D], F32, kind="ExternalInput").ap()
    wup_hbm = nc.dram_tensor("wup", [F, D], F32, kind="ExternalInput").ap()
    wdn_hbm = nc.dram_tensor("wdn", [D, F], F32, kind="ExternalInput").ap()

    ARENA = 212544
    arena_base, _ = nc.bump_sbuf(ARENA)
    off = [0]
    offs = {}

    def sb(name, shape, dtype, align=32, at=None):
        nbytes = int(np.prod(shape[1:])) * mybir.dt.size(dtype)
        if at is None:
            off[0] = (off[0] + align - 1) // align * align
            at = off[0]
            off[0] += nbytes
        offs[name] = at
        return nc.alloc_sbuf_tensor_at(name, list(shape), dtype,
                                       offset=arena_base + at).ap()

    wqupT = sb("wqupT", [128, 8 * F], BF16)
    wqdnT = sb("wqdnT", [128, 32 * D], BF16)
    gsT = sb("gsT", [128, 8 * G], F32)
    guidN = sb("guidN", [128, 4 * D], F32)
    xr = [sb(f"xr{i}", [128, D], F32) for i in range(3)]
    BIG = sb("BIGf", [128, 4096], F32)
    gfm_h = sb("gfm_h", [128, 1024], BF16, at=offs["BIGf"] + 8192)
    gfm_l = sb("gfm_l", [128, 1024], BF16, at=offs["BIGf"] + 10240)
    xnT_h = sb("xnT_h", [128, 1024], BF16)
    xnT_l = sb("xnT_l", [128, 1024], BF16)
    SEC_a = sb("SEC_a", [128, 1024], BF16)
    SEC_b = sb("SEC_b", [128, 1024], BF16)
    xr.append(sb("xr3", [128, D], F32, at=offs["SEC_a"]))
    SECFULL = sb("SECFULL", [128, 4096], BF16, at=offs["xnT_h"])
    astgU = sb("astgU", [128, 16384], F32, at=offs["wqupT"])
    astgD = sb("astgD", [128, 16384], F32, at=offs["wqdnT"])
    p32 = sb("p32", [128, G], F32, at=offs["BIGf"] + 8192)
    pT = sb("pT", [128, G], F32, at=offs["BIGf"] + 10240)
    er = [sb(f"er{i}", [128, D], F32) for i in range(2)]
    bandC = sb("bandC", [128, 128], F32)
    bandP = sb("bandP", [128, 128], F32)
    bandC0 = sb("bandC0", [128, 128], F32)
    identf = sb("identf", [128, 128], F32)
    mshift = sb("mshift", [128, 128], F32)
    identb = sb("identb", [128, 128], BF16)
    ln2c1r = [sb(f"ln2c1_{i}", [128, 8], F32) for i in range(4)]
    onesr = sb("onesr", [1, 128], F32, at=offs["er0"])
    magrow = sb("magrow", [1, 128], F32, at=offs["SEC_a"])
    j8 = sb("j8", [128, 8], F32, at=offs["SEC_a"])
    s8 = sb("s8", [128, 8], F32, at=offs["SEC_a"] + 32)
    onesf = sb("onesf", [128, 1], F32, align=32)
    ones1 = sb("ones1", [1, 1], F32, align=32)
    t2r = [sb(f"t2r{i}", [128, 1], F32, align=32) for i in range(2)]
    _colnames = ("sxc", "sqc", "mcol", "vcol", "acol", "bcol", "scol",
                 "sg0", "sg1", "sg2", "sg3", "sgc",
                 "magsq", "magc", "imc", "maskc", "anc", "rc", "alc", "bec",
                 "mxc", "nbc", "sumc", "invw", "tmp0", "tmp1", "tmp2",
                 "thrp", "wacc", "a2r0", "a2r1", "a2r2", "a2r3",
                 "b2r0", "b2r1", "b2r2", "b2r3", "eps5",
                 "hs0_0", "hs1_0", "hs0_1", "hs1_1")
    colpack = sb("colpack", [128, len(_colnames)], F32)
    cols = {n: colpack[:, i:i + 1] for i, n in enumerate(_colnames)}
    assert off[0] <= ARENA, f"SBUF over budget: {off[0]}"

    BS = [BIG[:, i * 1024:(i + 1) * 1024] for i in range(4)]

    PT_A = nc.alloc_psum_tensor("PT_A", [128, 1024], F32).ap()
    PT_B = nc.alloc_psum_tensor("PT_B", [128, 1024], F32).ap()
    PT_C = nc.alloc_psum_tensor("PT_C", [128, 512], F32).ap()
    PT_E = nc.alloc_psum_tensor("PT_E", [128, 512], F32).ap()
    PT_F = nc.alloc_psum_tensor("PT_F", [128, 512], F32).ap()
    PT_G = nc.alloc_psum_tensor("PT_G", [128, 512], F32).ap()

    em = Em(nc)

    bF = {}
    for name in ("wqupT", "wqdnT", "gsT", "guidN", "S0", "S1", "S2", "S3",
                 "xnT_h", "xnT_l", "SEC_a", "SEC_b", "band", "ident", "ones",
                 "ln2c1_0", "ln2c1_1", "ln2c1_2", "ln2c1_3", "t2r0", "t2r1", "msh",
                 "PT_A", "PT_Blo", "PT_Bhi", "PT_C", "PT_E", "PT_F", "PT_G",
                 "gfm_h", "gfm_l"):
        bF[name] = Buf()
    bX = [Buf() for _ in range(4)]
    bE = [Buf() for _ in range(2)]
    bC = {name: Buf() for name in cols}
    SECALL = (bF["xnT_h"], bF["xnT_l"], bF["SEC_a"], bF["SEC_b"])
    BIGALL = (bF["S0"], bF["S1"], bF["S2"], bF["S3"])

    def vop(fn, reads=(), writes=(), extra=()):
        return em.op("vector", fn, reads, writes, extra)

    def sop(fn, reads=(), writes=(), extra=()):
        return em.op("scalar", fn, reads, writes, extra)

    def gop(fn, reads=(), writes=(), extra=()):
        return em.op("gpsimd", fn, reads, writes, extra)

    def top(fn, reads=(), writes=(), extra=()):
        return em.op("tensor", fn, reads, writes, extra)

    def tgroup(fns, reads=(), writes=(), extra=()):
        return em.group("tensor", fns, reads, writes, extra)

    def sqrt_col(dst, srcname, bias=0.0, scale_after=0.5):
        if bias == 0.0:
            sop(lambda e: e.activation(cols[dst], cols[srcname], AF.Ln),
                reads=[bC[srcname]], writes=[bC[dst]])
        else:
            sop(lambda e: e.activation(cols[dst], cols[srcname], AF.Ln,
                                       bias=cols["eps5"]),
                reads=[bC[srcname], bC["eps5"]], writes=[bC[dst]])
        sop(lambda e: e.activation(cols[dst], cols[dst], AF.Exp,
                                   scale=scale_after),
            writes=[bC[dst]])


    def make_ident(t, bufkey):
        gop(lambda e, t=t: e.memset(t, 0.0), writes=[bF[bufkey]])
        gop(lambda e, t=t: e.affine_select(
            out=t, in_=t, pattern=[[-1, 128]], compare_op=ALU.not_equal,
            fill=1.0, base=0, channel_multiplier=1), writes=[bF[bufkey]])

    make_ident(identf, "ident")
    gop(lambda e: e.memset(mshift, 0.0), writes=[bF["msh"]])
    gop(lambda e: e.affine_select(
        out=mshift, in_=mshift, pattern=[[-1, 128]], compare_op=ALU.not_equal,
        fill=-1.0, base=0, channel_multiplier=1), writes=[bF["msh"]])
    gop(lambda e: e.affine_select(
        out=mshift, in_=mshift, pattern=[[-1, 128]], compare_op=ALU.not_equal,
        fill=1.0, base=-1, channel_multiplier=1), writes=[bF["msh"]])
    make_ident(identb, "ident")
    gop(lambda e: e.memset(onesf, 1.0), writes=[bF["ones"]])
    gop(lambda e: e.memset(ones1, 1.0), writes=[bF["ones"]])

    def band_tile(t, base_k, mask2_base, mask2_mult):
        gop(lambda e, t=t, b=base_k: e.iota(
            t, pattern=[[-1, 128]], base=b, channel_multiplier=1,
            allow_small_or_imprecise_dtypes=True), writes=[bF["band"]])
        vop(lambda e, t=t: e.tensor_scalar_mul(t, t, 0.9), writes=[bF["band"]])
        vop(lambda e, t=t: e.tensor_scalar_mul(t, t, 1.0 / 7.0), writes=[bF["band"]])
        vop(lambda e, t=t: e.tensor_scalar_add(t, t, 0.1), writes=[bF["band"]])
        gop(lambda e, t=t, b=base_k: e.affine_select(
            out=t, in_=t, pattern=[[-1, 128]], compare_op=ALU.is_ge,
            fill=0.0, base=b, channel_multiplier=1), writes=[bF["band"]])
        gop(lambda e, t=t, b=mask2_base, m=mask2_mult: e.affine_select(
            out=t, in_=t, pattern=[[1, 128]], compare_op=ALU.is_ge,
            fill=0.0, base=b, channel_multiplier=m), writes=[bF["band"]])

    band_tile(bandC, 8, -1, -1)
    band_tile(bandP, -120, 127, -1)
    vop(lambda e: e.tensor_copy(bandC0, bandC), reads=[bF["band"]], writes=[bF["band"]])
    gop(lambda e: e.iota(j8, pattern=[[0, 8]], base=0, channel_multiplier=1,
                         allow_small_or_imprecise_dtypes=True), writes=[bF["SEC_a"]])
    gop(lambda e: e.iota(s8, pattern=[[1, 8]], base=0, channel_multiplier=0,
                         allow_small_or_imprecise_dtypes=True), writes=[bF["SEC_a"]])
    vop(lambda e: e.tensor_scalar_sub(s8, s8, 1.0), writes=[bF["SEC_a"]])
    vop(lambda e: e.tensor_scalar_max(s8, s8, 1.0), writes=[bF["SEC_a"]])
    vop(lambda e: e.reciprocal(s8, s8), writes=[bF["SEC_a"]])
    vop(lambda e: e.tensor_scalar_mul(j8, j8, 0.9), writes=[bF["SEC_a"]])
    vop(lambda e: e.tensor_mul(j8, j8, s8), reads=[bF["SEC_a"]], writes=[bF["SEC_a"]])
    vop(lambda e: e.tensor_scalar_add(j8, j8, 0.1), writes=[bF["SEC_a"]])
    gop(lambda e: e.affine_select(out=j8, in_=j8, pattern=[[1, 8]],
                                  compare_op=ALU.is_ge, fill=0.0, base=-1,
                                  channel_multiplier=-1), writes=[bF["SEC_a"]])
    gop(lambda e: e.affine_select(out=j8, in_=j8, pattern=[[1, 8]],
                                  compare_op=ALU.is_ge, fill=0.0, base=-1,
                                  channel_multiplier=0), writes=[bF["SEC_a"]])
    vop(lambda e: e.tensor_copy(bandC0[:, 0:8], j8), reads=[bF["SEC_a"]], writes=[bF["band"]])

    gop(lambda e: e.memset(cols["eps5"], 1e-5), writes=[bC["eps5"]])

    _loaded = set()

    def load_x(c):
        if c < nch and c not in _loaded:
            _loaded.add(c)
            extra = []
            if c == 3:
                extra = [("p", "vector", em.cnt["vector"]),
                         ("p", "scalar", em.cnt["scalar"]),
                         ("p", "tensor", em.cnt["tensor"])]
            em.dma("sync", f"x{c % 4}", xr[c % 4],
                   x_hbm[c * 128:(c + 1) * 128, :],
                   writes=[bX[c % 4]], extra=extra)

    load_x(0)
    load_x(1)
    load_x(2)

    _gtoks = []
    for gc in range(4):
        for k in range(4):
            _gtoks.append(em.dma(
                "sync", f"wa{k}",
                guidN[:, gc * D + k * 256: gc * D + (k + 1) * 256],
                guid_hbm[gc * 128:(gc + 1) * 128, k * 256:(k + 1) * 256],
                chain=True))
    bF["guidN"].wtok = _gtoks
    bF["guidN"].rtoks = []
    for gc in range(4):
        gsl = guidN[:, gc * D:(gc + 1) * D]
        sop(lambda e, gsl=gsl: e.activation(BS[1], gsl, AF.Square,
                                            accum_out=cols["tmp0"]),
            reads=[bF["guidN"]], writes=[bF["S1"], bC["tmp0"]])
        vop(lambda e: e.tensor_scalar_max(cols["tmp0"], cols["tmp0"], 1e-30),
            writes=[bC["tmp0"]])
        sqrt_col("tmp0", "tmp0")
        vop(lambda e: e.tensor_scalar_max(cols["tmp0"], cols["tmp0"], 1e-8),
            writes=[bC["tmp0"]])
        vop(lambda e: e.reciprocal(cols["tmp1"], cols["tmp0"]),
            reads=[bC["tmp0"]], writes=[bC["tmp1"]])
        sop(lambda e, gsl=gsl: e.activation(BS[0], gsl, AF.Identity,
                                            scale=cols["tmp1"]),
            reads=[bF["guidN"], bC["tmp1"]], writes=[bF["S0"]])
        tgroup([
            (lambda e, dj=dj: e.transpose(PT_B[:, dj * 128:(dj + 1) * 128],
                                          BS[0][:, dj * 128:(dj + 1) * 128],
                                          identf))
            for dj in range(8)
        ], reads=[bF["S0"], bF["ident"]], writes=[bF["PT_Blo"], bF["PT_Bhi"]])
        em.group("scalar", [
            (lambda e, dj=dj, gc=gc: e.copy(
                gsT[:, dj * G + gc * 128: dj * G + (gc + 1) * 128],
                PT_B[:, dj * 128:(dj + 1) * 128]))
            for dj in range(8)
        ], reads=[bF["PT_Blo"], bF["PT_Bhi"]], writes=[bF["gsT"]])

    gop(lambda e: e.memset(onesr, 1.0), writes=[bE[0]])

    BIGH = [BIG[:, 0:2048], BIG[:, 2048:4096]]
    BIGH3 = [BIG.rearrange("p (j d) -> p j d", j=4)[:, 0:2, :],
             BIG.rearrange("p (j d) -> p j d", j=4)[:, 2:4, :]]
    SECH = [SECFULL[:, 0:2048], SECFULL[:, 2048:4096]]
    BHB = [(bF["S0"], bF["S1"]), (bF["S2"], bF["S3"])]
    SHB = [(bF["xnT_h"], bF["xnT_l"]), (bF["SEC_a"], bF["SEC_b"])]

    def wup_half_ap(p):
        return wup_hbm[p * 256:(p + 1) * 256, :].rearrange("(j p) d -> p j d", p=128)

    def wdn_half_ap(i):
        p, hh = i // 2, i % 2
        return wdn_hbm[p * 128:(p + 1) * 128, hh * 2048:(hh + 1) * 2048]

    def alpha_memset():
        vop(lambda e: e.memset(cols["wacc"], 0.0), writes=[bC["wacc"]])

    def alpha_piece(i, is3d, ap, astg, slots, use_s=False):
        sl = i % 8
        dst2 = astg[:, sl * 2048:(sl + 1) * 2048]
        prior = list(slots[sl].rtoks) + [slots[sl].wtok]
        toks = []
        for k in range(4):
            if is3d:
                dst3 = dst2.rearrange("p (j d) -> p j d", j=2)
                d = dst3[:, k // 2:k // 2 + 1, (k % 2) * 512:(k % 2 + 1) * 512]
                s = ap[:, k // 2:k // 2 + 1, (k % 2) * 512:(k % 2 + 1) * 512]
            else:
                d = dst2[:, k * 512:(k + 1) * 512]
                s = ap[:, k * 512:(k + 1) * 512]
            toks.append(em.dma("sync", f"wa{k}", d, s, extra=prior, chain=True))
        slots[sl].wtok = toks
        slots[sl].rtoks = []
        tc = cols["tmp1"] if i % 2 else cols["tmp0"]
        tb = bC["tmp1"] if i % 2 else bC["tmp0"]
        if use_s:
            sop(lambda e, dst2=dst2, tc=tc: e.activation(
                dst2, dst2, AF.Abs, accum_out=tc),
                reads=[slots[sl]], writes=[slots[sl], tb])
        else:
            vop(lambda e, dst2=dst2, tc=tc: e.tensor_reduce(
                tc, dst2, AX.X, ALU.add, apply_absolute_value=True),
                reads=[slots[sl]], writes=[tb])
        vop(lambda e, tc=tc: e.tensor_add(cols["wacc"], cols["wacc"], tc),
            reads=[tb], writes=[bC["wacc"]])

    def alpha_thr():
        top(lambda e: e.matmul(PT_C[0:1, 0:1], cols["wacc"], onesf,
                               start=True, stop=True),
            reads=[bC["wacc"], bF["ones"]], writes=[bF["PT_C"]])
        vop(lambda e: e.tensor_copy(cols["tmp0"][0:1, :], PT_C[0:1, 0:1]),
            reads=[bF["PT_C"]], writes=[bC["tmp0"]])
        vop(lambda e: e.tensor_scalar_mul(cols["tmp0"][0:1, :],
                                          cols["tmp0"][0:1, :],
                                          0.5 / float(F * D)), writes=[bC["tmp0"]])
        top(lambda e: e.matmul(PT_C[:, 300:301], onesr, cols["tmp0"][0:1, :],
                               start=True, stop=True),
            reads=[bE[0], bC["tmp0"]], writes=[bF["PT_C"]])
        vop(lambda e: e.tensor_copy(cols["thrp"], PT_C[:, 300:301]),
            reads=[bF["PT_C"]], writes=[bC["thrp"]])

    SECQ = [SECFULL[:, q * 1024:(q + 1) * 1024] for q in range(4)]
    SQB = [bF["xnT_h"], bF["xnT_l"], bF["SEC_a"], bF["SEC_b"]]
    BSB = [bF["S0"], bF["S1"], bF["S2"], bF["S3"]]

    def tern_dma_q(p, src_ap):
        s = p % 4
        prior = list(BSB[s].rtoks) + [BSB[s].wtok]
        toks = []
        for k in range(4):
            toks.append(em.dma("sync", f"wc{k}",
                               BS[s][:, k * 256:(k + 1) * 256],
                               src_ap[:, k * 256:(k + 1) * 256],
                               extra=prior, chain=True))
        BSB[s].wtok = toks
        BSB[s].rtoks = []

    def sign_abs_q(p):
        s = p % 4
        sop(lambda e, s=s: e.activation(SECQ[s], BS[s], AF.Sign),
            reads=[BSB[s]], writes=[SQB[s]])
        sop(lambda e, s=s: e.activation(BS[s], BS[s], AF.Abs),
            writes=[BSB[s]])

    def isgt_mul_q(p):
        s = p % 4
        vop(lambda e, s=s: e.tensor_scalar(BS[s], BS[s], cols["thrp"],
                                           None, ALU.is_gt),
            reads=[bC["thrp"]], writes=[BSB[s]])
        vop(lambda e, s=s: e.tensor_mul(SECQ[s], SECQ[s], BS[s]),
            reads=[BSB[s]], writes=[SQB[s]])

    def wq_tr_evac_q(p, dsts, wbuf, aslots):
        s = p % 4
        pts = ((PT_F, bF["PT_F"]), (PT_G, bF["PT_G"])) if p % 2 == 0 else \
              ((PT_C, bF["PT_C"]), (PT_E, bF["PT_E"]))
        tgroup([
            (lambda e, s=s, dj=dj: e.matmul(
                pts[dj // 4][0][:, (dj % 4) * 128:(dj % 4 + 1) * 128],
                SECQ[s][:, dj * 128:(dj + 1) * 128],
                identb, start=True, stop=True))
            for dj in range(8)
        ], reads=[SQB[s], bF["ident"]], writes=[pts[0][1], pts[1][1]])
        eng = "vector" if p % 2 == 0 else "scalar"
        em.group(eng, [
            (lambda e, dj=dj, d=d: (e.tensor_copy(d, pts[dj // 4][0][:, (dj % 4) * 128:(dj % 4 + 1) * 128])
                                    if eng == "vector" else
                                    e.copy(d, pts[dj // 4][0][:, (dj % 4) * 128:(dj % 4 + 1) * 128])))
            for dj, d in enumerate(dsts)
        ], reads=[pts[0][1], pts[1][1]], writes=[wbuf] + aslots)

    upslots = [Buf() for _ in range(8)]
    dnslots = [Buf() for _ in range(8)]
    alpha_memset()
    for p in range(16):
        alpha_piece(p, True, wup_half_ap(p), astgU, upslots, use_s=(p % 2 == 1))
    alpha_thr()
    alpha_memset()
    tern_dma_q(0, wup_hbm[0:128, :])
    tern_dma_q(1, wup_hbm[128:256, :])
    sign_abs_q(0)
    for p in range(32):
        if p + 2 < 32:
            tern_dma_q(p + 2, wup_hbm[(p + 2) * 128:(p + 3) * 128, :])
        if p + 1 < 32:
            sign_abs_q(p + 1)
        if p % 2 == 0:
            alpha_piece(p // 2, False, wdn_half_ap(p // 2), astgD, dnslots)
        isgt_mul_q(p)
        wq_tr_evac_q(p,
                     [wqupT[:, dj * F + p * 128: dj * F + (p + 1) * 128]
                      for dj in range(8)],
                     wbuf=bF["wqupT"], aslots=upslots)
    alpha_thr()
    def wdn_q_ap(i):
        return wdn_hbm[(i // 4) * 128:(i // 4 + 1) * 128,
                       (i % 4) * 1024:(i % 4 + 1) * 1024]

    tern_dma_q(0, wdn_q_ap(0))
    tern_dma_q(1, wdn_q_ap(1))
    sign_abs_q(0)
    for i in range(32):
        if i + 2 < 32:
            tern_dma_q(i + 2, wdn_q_ap(i + 2))
        if i + 1 < 32:
            sign_abs_q(i + 1)
        isgt_mul_q(i)
        r, q = i // 4, i % 4
        wq_tr_evac_q(i,
                     [wqdnT[:, (q * 8 + dj) * D + r * 128: (q * 8 + dj) * D + (r + 1) * 128]
                      for dj in range(8)],
                     wbuf=bF["wqdnT"], aslots=dnslots)

    for _g in ("gfm_h", "gfm_l"):
        bF[_g].wtok = bF["S2"].wtok
        bF[_g].rtoks = list(bF["S2"].rtoks)

    def ln_stats(xa, xbuf, fused=False, par=0):
        if fused:
            h0, h1 = f"hs0_{par}", f"hs1_{par}"
            vop(lambda e, h0=h0, h1=h1: e.tensor_add(cols["sxc"], cols[h0],
                                                     cols[h1]),
                reads=[bC[h0], bC[h1]], writes=[bC["sxc"]])
            sop(lambda e, xa=xa: e.activation(BS[2], xa, AF.Square,
                                              accum_out=cols["sqc"]),
                reads=[xbuf], writes=[bF["S2"], bF["gfm_h"], bF["gfm_l"],
                                      bC["sqc"]])
        else:
            vop(lambda e, xa=xa: e.reduce_sum(cols["sxc"], xa, AX.X),
                reads=[xbuf], writes=[bC["sxc"]])
            vop(lambda e, xa=xa: e.scalar_tensor_tensor(
                er[0], xa, 1.0, xa, ALU.mult, ALU.mult,
                accum_out=cols["sqc"]),
                reads=[xbuf], writes=[bE[0], bC["sqc"]])
        vop(lambda e: e.tensor_scalar_mul(cols["mcol"], cols["sxc"], 1.0 / D),
            reads=[bC["sxc"]], writes=[bC["mcol"]])
        vop(lambda e: e.tensor_mul(cols["tmp0"], cols["mcol"], cols["mcol"]),
            reads=[bC["mcol"]], writes=[bC["tmp0"]])
        vop(lambda e: e.tensor_scalar(cols["vcol"], cols["sqc"], 1.0 / D,
                                      cols["tmp0"], ALU.mult, ALU.subtract),
            reads=[bC["sqc"], bC["tmp0"]], writes=[bC["vcol"]])

    def ln_affine_ffn():
        vop(lambda e: e.tensor_scalar_max(cols["tmp1"], cols["vcol"], 1e-30),
            reads=[bC["vcol"]], writes=[bC["tmp1"]])
        sqrt_col("tmp1", "tmp1")
        sop(lambda e: e.activation(cols["tmp2"], cols["vcol"], AF.Ln,
                                   bias=cols["eps5"]),
            reads=[bC["vcol"], bC["eps5"]], writes=[bC["tmp2"]])
        sop(lambda e: e.activation(cols["tmp2"], cols["tmp2"], AF.Exp,
                                   scale=0.5), writes=[bC["tmp2"]])
        vop(lambda e: e.scalar_tensor_tensor(cols["tmp0"], cols["tmp2"], 1e-8,
                                             cols["tmp1"], ALU.mult, ALU.add),
            reads=[bC["tmp1"], bC["tmp2"]], writes=[bC["tmp0"]])
        vop(lambda e: e.reciprocal(cols["acol"], cols["tmp0"]),
            reads=[bC["tmp0"]], writes=[bC["acol"]])
        vop(lambda e: e.scalar_tensor_tensor(cols["bcol"], cols["mcol"], -1.0,
                                             cols["acol"], ALU.mult, ALU.mult),
            reads=[bC["mcol"], bC["acol"]], writes=[bC["bcol"]])

    def ln_affine_plain(da, db):
        sop(lambda e: e.activation(cols["tmp0"], cols["vcol"], AF.Ln,
                                   bias=cols["eps5"]),
            reads=[bC["vcol"], bC["eps5"]], writes=[bC["tmp0"]])
        sop(lambda e: e.activation(cols["tmp0"], cols["tmp0"], AF.Exp,
                                   scale=0.5), writes=[bC["tmp0"]])
        vop(lambda e, da=da: e.reciprocal(cols[da], cols["tmp0"]),
            reads=[bC["tmp0"]], writes=[bC[da]])
        vop(lambda e, da=da, db=db: e.scalar_tensor_tensor(
            cols[db], cols["mcol"], -1.0, cols[da], ALU.mult, ALU.mult),
            reads=[bC["mcol"], bC[da]], writes=[bC[db]])

    def ffn_sc(c, fused=False):
        xa = xr[c % 4]
        xbuf = bX[c % 4]
        ln_stats(xa, xbuf, fused=fused, par=c % 2)
        ln_affine_ffn()
        sop(lambda e, xa=xa: e.activation(BS[1], xa, AF.Identity,
                                          scale=cols["acol"], bias=cols["bcol"]),
            reads=[xbuf, bC["acol"], bC["bcol"]], writes=[bF["S1"]])

    def trsplit(src, srcbuf, dst_h, dst_l, dstbufs, cast_s=False):
        pbs = (bF["PT_Blo"], bF["PT_Bhi"])
        for half in range(2):
            tgroup([
                (lambda e, src=src, dj=dj: e.transpose(
                    PT_B[:, dj * 128:(dj + 1) * 128],
                    src[:, dj * 128:(dj + 1) * 128], identf))
                for dj in range(half * 4, half * 4 + 4)
            ], reads=[srcbuf, bF["ident"]], writes=[pbs[half]])
        for half in range(2):
            if cast_s:
                sop(lambda e, dst_h=dst_h, half=half: e.copy(
                    dst_h[:, half * 512:(half + 1) * 512],
                    PT_B[:, half * 512:(half + 1) * 512]),
                    reads=[pbs[half]], writes=[dstbufs[0]])
            else:
                vop(lambda e, dst_h=dst_h, half=half: e.tensor_copy(
                    dst_h[:, half * 512:(half + 1) * 512],
                    PT_B[:, half * 512:(half + 1) * 512]),
                    reads=[pbs[half]], writes=[dstbufs[0]])
        for half in range(2):
            vop(lambda e, dst_h=dst_h, dst_l=dst_l, half=half: e.tensor_sub(
                dst_l[:, half * 512:(half + 1) * 512],
                PT_B[:, half * 512:(half + 1) * 512],
                dst_h[:, half * 512:(half + 1) * 512]),
                reads=[pbs[half], dstbufs[0]], writes=[dstbufs[1]])

    def ffn_tr(c):
        trsplit(BS[1], bF["S1"], xnT_h, xnT_l, (bF["xnT_h"], bF["xnT_l"]),
                cast_s=True)

    def ffn_pe(c):
        xa = xr[c % 4]
        xbuf = bX[c % 4]
        def gq(q):
            return (BS[0], bF["S0"]) if q % 2 == 0 else (BS[1], bF["S1"])

        def quarter_split_tr(q):
            gsrc, gbuf = gq(q)
            trsplit(gsrc, gbuf, gfm_h, gfm_l, (bF["gfm_h"], bF["gfm_l"]),
                    cast_s=True)
            vop(lambda e, gsrc=gsrc, q=q: e.scalar_tensor_tensor(
                er[0], gsrc, 1.0, gsrc, ALU.mult, ALU.mult,
                accum_out=cols[f"sg{q}"]),
                reads=[gbuf], writes=[bE[0], bC[f"sg{q}"]])

        def quarter_mm2(q):
            for pi, (src, srcb) in enumerate(((gfm_h, bF["gfm_h"]),
                                              (gfm_l, bF["gfm_l"]))):
                fns = []
                for i in range(8):
                    fg = q * 8 + i
                    for nh in range(2):
                        fns.append(lambda e, src=src, i=i, fg=fg, nh=nh, pi=pi, q=q:
                                   e.matmul(PT_A[:, nh * 512:(nh + 1) * 512],
                                            src[:, i * 128:(i + 1) * 128],
                                            wqdnT[:, fg * D + nh * 512: fg * D + (nh + 1) * 512],
                                            start=(q == 0 and pi == 0 and i == 0),
                                            stop=(q == 3 and pi == 1 and i == 7)))
                tgroup(fns, reads=[srcb, bF["wqdnT"]], writes=[bF["PT_A"]])

        mm1rot = ((PT_C, bF["PT_C"]), (PT_E, bF["PT_E"]),
                  (PT_F, bF["PT_F"]), (PT_G, bF["PT_G"]))
        for e8 in range(8):
            pt, ptb = mm1rot[e8 % 4]
            for pi, (src, srcb) in enumerate(((xnT_h, bF["xnT_h"]),
                                              (xnT_l, bF["xnT_l"]))):
                fns = []
                for dj in range(8):
                    fns.append(lambda e, src=src, dj=dj, pt=pt, e8=e8, pi=pi:
                               e.matmul(pt,
                                        src[:, dj * 128:(dj + 1) * 128],
                                        wqupT[:, dj * F + e8 * 512: dj * F + (e8 + 1) * 512],
                                        start=(pi == 0 and dj == 0),
                                        stop=(pi == 1 and dj == 7)))
                tgroup(fns, reads=[srcb, bF["wqupT"]], writes=[ptb])
            gsrc, gbuf = gq(e8 // 2)
            sop(lambda e, pt=pt, e8=e8, gsrc=gsrc: e.activation(
                gsrc[:, (e8 % 2) * 512:(e8 % 2 + 1) * 512], pt, AF.Gelu),
                reads=[ptb], writes=[gbuf])
            if e8 >= 3 and e8 % 2 == 1:
                q = (e8 - 3) // 2
                if q >= 1:
                    quarter_mm2(q - 1)
                quarter_split_tr(q)
        quarter_mm2(2)
        quarter_split_tr(3)
        quarter_mm2(3)
        vop(lambda e: e.tensor_add(cols["sgc"], cols["sg0"], cols["sg1"]),
            reads=[bC["sg0"], bC["sg1"]], writes=[bC["sgc"]])
        vop(lambda e: e.tensor_add(cols["sgc"], cols["sgc"], cols["sg2"]),
            reads=[bC["sg2"]], writes=[bC["sgc"]])
        vop(lambda e: e.tensor_add(cols["sgc"], cols["sgc"], cols["sg3"]),
            reads=[bC["sg3"]], writes=[bC["sgc"]])
        vop(lambda e: e.tensor_scalar_max(cols["tmp0"], cols["sgc"], 1e-30),
            reads=[bC["sgc"]], writes=[bC["tmp0"]])
        sqrt_col("tmp0", "tmp0")
        vop(lambda e: e.tensor_scalar(cols["tmp0"], cols["tmp0"], 1.0 / 64.0,
                                      1e-8, ALU.mult, ALU.add),
            writes=[bC["tmp0"]])
        vop(lambda e: e.reciprocal(cols["scol"], cols["tmp0"]),
            reads=[bC["tmp0"]], writes=[bC["scol"]])
        vop(lambda e: e.tensor_scalar_mul(cols["scol"], cols["scol"], 0.5),
            writes=[bC["scol"]])
        for nh in range(2):
            vop(lambda e, xa=xa, nh=nh: e.scalar_tensor_tensor(
                xa[:, nh * 512:(nh + 1) * 512], PT_A[:, nh * 512:(nh + 1) * 512],
                cols["scol"], xa[:, nh * 512:(nh + 1) * 512], ALU.mult, ALU.add,
                accum_out=cols[f"hs{nh}_{c % 2}"]),
                reads=[bF["PT_A"], bC["scol"]],
                writes=[xbuf, bC[f"hs{nh}_{c % 2}"]])

    def attn_stats(c):
        ln_stats(xr[c % 4], bX[c % 4], fused=True, par=c % 2)
        ln_affine_plain(f"a2r{c % 4}", f"b2r{c % 4}")

    def attn_apply(c, row0=False):
        xa = xr[c % 4]
        a, b = f"a2r{c % 4}", f"b2r{c % 4}"
        if row0:
            sop(lambda e, xa=xa, a=a, b=b: e.activation(
                BS[0][0:1, :], xa[0:1, :], AF.Identity,
                scale=cols[a][0:1, :], bias=cols[b][0:1, :]),
                reads=[bX[c % 4], bC[a], bC[b]], writes=[bF["S0"]])
        else:
            sop(lambda e, xa=xa, a=a, b=b: e.activation(
                BS[0], xa, AF.Identity, scale=cols[a], bias=cols[b]),
                reads=[bX[c % 4], bC[a], bC[b]], writes=[bF["S0"]])

    def attn_row0(c1):
        tgroup([
            (lambda e, dj=dj: e.matmul(PT_C[:, dj:dj + 1],
                                       BS[0][0:1, dj * 128:(dj + 1) * 128],
                                       ones1, start=True, stop=True))
            for dj in range(8)
        ], reads=[bF["S0"], bF["ones"]], writes=[bF["PT_C"]])
        vop(lambda e, c1=c1: e.tensor_copy(ln2c1r[c1 % 4], PT_C[:, 0:8]),
            reads=[bF["PT_C"]], writes=[bF[f"ln2c1_{c1 % 4}"]])

    def attn_front_pe(c):
        attn_apply(c)
        tgroup([
            (lambda e, dj=dj: e.matmul(PT_B[:, dj * 128:(dj + 1) * 128],
                                       BS[0][:, dj * 128:(dj + 1) * 128],
                                       mshift, start=True, stop=True))
            for dj in range(8)
        ], reads=[bF["S0"], bF["msh"]], writes=[bF["PT_Blo"], bF["PT_Bhi"]])
        vop(lambda e: e.tensor_copy(BS[3], PT_B),
            reads=[bF["PT_Blo"], bF["PT_Bhi"]], writes=[bF["S3"]])

    def attn_back(c):
        xa = xr[c % 4]
        xbuf = bX[c % 4]
        if c + 1 < nch:
            ln2c1 = ln2c1r[(c + 1) % 4]
            lb = bF[f"ln2c1_{(c + 1) % 4}"]
            for dj in range(8):
                vop(lambda e, dj=dj, ln2c1=ln2c1: e.tensor_add(
                    BS[3][:, dj * 128 + 127: dj * 128 + 128],
                    BS[3][:, dj * 128 + 127: dj * 128 + 128],
                    ln2c1[:, dj:dj + 1]),
                    reads=[lb], writes=[bF["S3"]])
        else:
            for dj in range(8):
                vop(lambda e, dj=dj: e.memset(
                    BS[3][:, dj * 128 + 127: dj * 128 + 128], 0.0),
                    writes=[bF["S3"]])
        sop(lambda e: e.activation(BS[2], BS[3], AF.Square),
            reads=[bF["S3"]], writes=[bF["S2"], bF["gfm_h"], bF["gfm_l"]])
        tgroup([
            (lambda e, dj=dj: e.matmul(PT_E[:, 200:201],
                                       BS[2][:, dj * 128:(dj + 1) * 128],
                                       onesf, start=(dj == 0), stop=(dj == 7)))
            for dj in range(8)
        ], reads=[bF["S2"], bF["ones"]], writes=[bF["PT_E"]])
        vop(lambda e: e.tensor_scalar_max(cols["magsq"], PT_E[:, 200:201],
                                          1e-30),
            reads=[bF["PT_E"]], writes=[bC["magsq"]])
        t2 = t2r[c % 2]
        t2b = bF[f"t2r{c % 2}"]
        sop(lambda e: e.activation(cols["magc"], cols["magsq"], AF.Ln),
            reads=[bC["magsq"]], writes=[bC["magc"]])
        sop(lambda e: e.activation(cols["magc"], cols["magc"], AF.Exp,
                                   scale=0.5), writes=[bC["magc"]])
        sop(lambda e: e.activation(cols["tmp2"], cols["magc"], AF.Exp,
                                   scale=-4.0),
            reads=[bC["magc"]], writes=[bC["tmp2"]])
        vop(lambda e: e.tensor_scalar_add(cols["tmp1"], cols["tmp2"], 1.0),
            reads=[bC["tmp2"]], writes=[bC["tmp1"]])
        vop(lambda e: e.reciprocal(cols["tmp1"], cols["tmp1"]),
            writes=[bC["tmp1"]])
        vop(lambda e: e.tensor_scalar(cols["tmp2"], cols["tmp2"], -1.0, 1.0,
                                      ALU.mult, ALU.add), writes=[bC["tmp2"]])
        vop(lambda e, t2=t2: e.tensor_mul(t2, cols["tmp2"], cols["tmp1"]),
            reads=[bC["tmp1"], bC["tmp2"]], writes=[t2b])
        vop(lambda e: e.tensor_scalar_max(cols["tmp0"], cols["magc"], 1e-30),
            reads=[bC["magc"]], writes=[bC["tmp0"]])
        vop(lambda e: e.reciprocal(cols["imc"], cols["tmp0"]),
            reads=[bC["tmp0"]], writes=[bC["imc"]])
        vop(lambda e: e.tensor_scalar(cols["maskc"], cols["magc"], 1e-6, None,
                                      ALU.is_gt),
            reads=[bC["magc"]], writes=[bC["maskc"]])
        vop(lambda e: e.tensor_mul(cols["anc"], cols["magc"], cols["imc"]),
            reads=[bC["magc"], bC["imc"]], writes=[bC["anc"]])
        vop(lambda e: e.tensor_scalar_max(cols["anc"], cols["anc"], 1e-8),
            writes=[bC["anc"]])
        vop(lambda e: e.reciprocal(cols["tmp1"], cols["anc"]),
            reads=[bC["anc"]], writes=[bC["tmp1"]])
        vop(lambda e: e.tensor_mul(cols["rc"], cols["imc"], cols["tmp1"]),
            reads=[bC["imc"], bC["tmp1"]], writes=[bC["rc"]])
        vop(lambda e: e.tensor_scalar(cols["rc"], cols["rc"], 2.0,
                                      cols["maskc"], ALU.mult, ALU.mult),
            reads=[bC["maskc"]], writes=[bC["rc"]])
        vop(lambda e, t2=t2: e.tensor_mul(cols["alc"], t2, cols["imc"]),
            reads=[t2b, bC["imc"]], writes=[bC["alc"]])
        vop(lambda e: e.tensor_scalar(cols["alc"], cols["alc"], 0.6,
                                      cols["maskc"], ALU.mult, ALU.mult),
            reads=[bC["maskc"]], writes=[bC["alc"]])
        tgroup([
            (lambda e, dj=dj: e.matmul(PT_C, BS[3][:, dj * 128:(dj + 1) * 128],
                                       gsT[:, dj * G:(dj + 1) * G],
                                       start=(dj == 0), stop=(dj == 7)))
            for dj in range(8)
        ], reads=[bF["S3"], bF["gsT"]], writes=[bF["PT_C"]])
        tgroup([
            (lambda e, dj=dj: e.transpose(PT_A[:, dj * 128:(dj + 1) * 128],
                                          BS[3][:, dj * 128:(dj + 1) * 128],
                                          identf))
            for dj in range(8)
        ], reads=[bF["S3"], bF["ident"]], writes=[bF["PT_A"]])
        vop(lambda e: e.reduce_max(cols["mxc"], PT_C, AX.X),
            reads=[bF["PT_C"]], writes=[bC["mxc"]])
        vop(lambda e: e.scalar_tensor_tensor(cols["nbc"], cols["rc"], -1.0,
                                             cols["mxc"], ALU.mult, ALU.mult),
            reads=[bC["rc"], bC["mxc"]], writes=[bC["nbc"]])
        sop(lambda e: e.activation(p32, PT_C, AF.Exp, scale=cols["rc"],
                                   bias=cols["nbc"], accum_out=cols["sumc"]),
            reads=[bF["PT_C"], bC["rc"], bC["nbc"]],
            writes=[bF["S2"], bF["gfm_h"], bC["sumc"]])
        vop(lambda e: e.reciprocal(cols["tmp0"], cols["sumc"]),
            reads=[bC["sumc"]], writes=[bC["tmp0"]])
        vop(lambda e, t2=t2: e.scalar_tensor_tensor(cols["bec"], t2, 0.4,
                                                    cols["tmp0"], ALU.mult,
                                                    ALU.mult),
            reads=[t2b, bC["tmp0"]], writes=[bC["bec"]])
        tgroup([
            (lambda e, gc=gc: e.transpose(PT_E[:, gc * 128:(gc + 1) * 128],
                                          p32[:, gc * 128:(gc + 1) * 128],
                                          identf))
            for gc in range(4)
        ], reads=[bF["gfm_h"], bF["ident"]], writes=[bF["PT_E"]])
        sop(lambda e: e.copy(pT, PT_E), reads=[bF["PT_E"]],
            writes=[bF["S2"], bF["gfm_l"]])
        fns = []
        for gc in range(4):
            for nh in range(2):
                fns.append(lambda e, gc=gc, nh=nh: e.matmul(
                    PT_B[:, nh * 512:(nh + 1) * 512],
                    pT[:, gc * 128:(gc + 1) * 128],
                    guidN[:, gc * D + nh * 512: gc * D + (nh + 1) * 512],
                    start=(gc == 0), stop=(gc == 3)))
        tgroup(fns, reads=[bF["gfm_l"], bF["guidN"]], writes=[bF["PT_Blo"], bF["PT_Bhi"]])
        ec = er[c % 2]
        ecb = bE[c % 2]
        vop(lambda e, ec=ec: e.tensor_scalar(ec, PT_A, cols["alc"], None, ALU.mult),
            reads=[bF["PT_A"], bC["alc"]], writes=[ecb])
        vop(lambda e, ec=ec: e.scalar_tensor_tensor(ec, PT_B, cols["bec"], ec,
                                                    ALU.mult, ALU.add),
            reads=[bF["PT_Blo"], bF["PT_Bhi"]], writes=[ecb])
        ep = er[(c - 1) % 2]
        epb = bE[(c - 1) % 2]
        t2p = t2r[(c - 1) % 2]
        t2pb = bF[f"t2r{(c - 1) % 2}"]
        tileC = bandC0 if c == 0 else bandC
        for pt, ptb, lo in ((PT_C, bF["PT_C"], 0), (PT_E, bF["PT_E"], 512)):
            fns = []
            if c > 0:
                fns.append(lambda e, pt=pt, lo=lo, ep=ep: e.matmul(
                    pt, bandP, ep[:, lo:lo + 512], start=True, stop=False))
            fns.append(lambda e, pt=pt, lo=lo, ec=ec, tileC=tileC: e.matmul(
                pt, tileC, ec[:, lo:lo + 512], start=(c == 0), stop=True))
            tgroup(fns, reads=[bF["band"], ecb] + ([epb] if c > 0 else []),
                   writes=[ptb])
        fns = []
        if c > 0:
            fns.append(lambda e, t2p=t2p: e.matmul(
                PT_B[:, 0:1], bandP, t2p, start=True, stop=False))
        fns.append(lambda e, t2=t2, tileC=tileC: e.matmul(
            PT_B[:, 0:1], tileC, t2, start=(c == 0), stop=True))
        tgroup(fns, reads=[bF["band"], t2b] + ([t2pb] if c > 0 else []),
               writes=[bF["PT_Blo"]])
        vop(lambda e: e.tensor_scalar_max(cols["tmp0"], PT_B[:, 0:1], 1e-8),
            reads=[bF["PT_Blo"]], writes=[bC["tmp0"]])
        vop(lambda e: e.reciprocal(cols["invw"], cols["tmp0"]),
            reads=[bC["tmp0"]], writes=[bC["invw"]])
        for nh, (pt, ptb, lo) in enumerate(((PT_C, bF["PT_C"], 0),
                                            (PT_E, bF["PT_E"], 512))):
            vop(lambda e, xa=xa, pt=pt, lo=lo, nh=nh: e.scalar_tensor_tensor(
                xa[:, lo:lo + 512], pt, cols["invw"], xa[:, lo:lo + 512],
                ALU.mult, ALU.add, accum_out=cols[f"hs{nh}_{c % 2}"]),
                reads=[ptb, bC["invw"]],
                writes=[xbuf, bC[f"hs{nh}_{c % 2}"]])

    def warm(dv=0):
        tok = ("p", "vector", max(0, em.cnt["vector"] - dv))
        em.op("tensor",
              lambda e: e.matmul(PT_C, identb, wqupT[:, 0:512],
                                 start=True, stop=True),
              reads=[bF["ident"], bF["wqupT"]], writes=[bF["PT_C"]],
              extra=[tok])

    def store(c):
        em.dma("gpsimd", f"st{c % 4}", out_hbm[c * 128:(c + 1) * 128, :],
               xr[c % 4], reads=[bX[c % 4]])

    assert nch % 2 == 0
    load_x(0)
    load_x(1)
    load_x(2)
    load_x(3)
    ffn_sc(0)
    ffn_tr(0)
    ffn_pe(0)
    attn_stats(0)
    ffn_sc(1)
    ffn_tr(1)
    ffn_pe(1)
    attn_stats(1)
    attn_apply(1, row0=True)
    attn_row0(1)
    for c in range(0, nch, 2):
        n2, n3 = c + 2 < nch, c + 3 < nch
        if n2:
            ffn_sc(c + 2)
            ffn_tr(c + 2)
        attn_front_pe(c)
        if n2:
            ffn_pe(c + 2)
            if n3:
                ffn_sc(c + 3)
            attn_stats(c + 2)
            attn_apply(c + 2, row0=True)
        if n3:
            ffn_tr(c + 3)
        if n2:
            attn_row0(c + 2)
        if n3:
            ffn_pe(c + 3)
            attn_stats(c + 3)
            attn_apply(c + 3, row0=True)
            attn_row0(c + 3)
        attn_back(c)
        attn_front_pe(c + 1)
        ffn_sc(c, fused=True)
        attn_back(c + 1)
        ffn_tr(c)
        ffn_pe(c)
        store(c)
        ffn_sc(c + 1, fused=True)
        ffn_tr(c + 1)
        ffn_pe(c + 1)
        store(c + 1)
        if n2:
            load_x(c + 4)
        if n3:
            load_x(c + 5)

    em.op("gpsimd", lambda e: e.nop(),
          extra=[em.dma_total("gpsimd", f"st{i}") for i in range(4)])

    em.replay()
    return nc


def kernel(L0_attn_embeddings, L0_attn_guidance_trajectories, L0_ffn_w_up,
           L0_ffn_w_down):
    x = np.ascontiguousarray(np.asarray(L0_attn_embeddings, dtype=np.float32))
    guid = np.ascontiguousarray(np.asarray(L0_attn_guidance_trajectories, dtype=np.float32))
    wup = np.ascontiguousarray(np.asarray(L0_ffn_w_up, dtype=np.float32))
    wdn = np.ascontiguousarray(np.asarray(L0_ffn_w_down, dtype=np.float32))
    nc = build(nch=S // 128)
    in_maps = [{"x": np.ascontiguousarray(x[b]), "guid": guid, "wup": wup,
                "wdn": wdn} for b in range(NCORE)]
    res = run_bass_kernel_spmd(nc, in_maps, core_ids=list(range(NCORE)))
    return np.stack([res.results[b]["out"] for b in range(NCORE)], axis=0)


# version 50
# speedup vs baseline: 1.0098x; 1.0098x over previous
import numpy as np
import concourse.bass as bass
import concourse.mybir as mybir
from concourse.bass_utils import run_bass_kernel_spmd

F32 = mybir.dt.float32
BF16 = mybir.dt.bfloat16
AF = mybir.ActivationFunctionType
ALU = mybir.AluOpType
AX = mybir.AxisListType

S, D, G, F = 4096, 1024, 512, 4096
NCORE = 8



class Buf:

    __slots__ = ("wtok", "rtoks")

    def __init__(self):
        self.wtok = None
        self.rtoks = []


class Em:
    ENGS = ("sync", "gpsimd", "scalar", "vector", "tensor")

    def __init__(self, nc):
        self.nc = nc
        self.q = {e: [] for e in self.ENGS}
        self.sem = {e: nc.alloc_semaphore(f"prog_{e}") for e in self.ENGS}
        self.cnt = {e: 0 for e in self.ENGS}
        self.dsem = {}
        self.dcnt = {}
        self.wm = {e: {} for e in self.ENGS}

    def _collect(self, reads, writes, extra):
        deps = [b.wtok for b in reads]
        for b in writes:
            deps.extend(b.rtoks)
            deps.append(b.wtok)
        deps.extend(extra)
        out = []
        for t in deps:
            if isinstance(t, list):
                out.extend(t)
            else:
                out.append(t)
        return out

    def _waits(self, eng, deps):
        waits = []
        best = {}
        for tok in deps:
            if tok is None:
                continue
            kind, src, val = tok
            key = (kind, src)
            if val > best.get(key, 0):
                best[key] = val
        for key, val in best.items():
            kind, src = key
            if val > self.wm[eng].get(key, 0):
                sem = self.sem[src] if kind == "p" else self.dsem[src]
                waits.append((sem, val if kind == "p" else 16 * val))
                self.wm[eng][key] = val
        return waits

    def _record(self, tok, reads, writes):
        for b in reads:
            b.rtoks.append(tok)
        for b in writes:
            b.wtok = tok
            b.rtoks = []

    def op(self, eng, fn, reads=(), writes=(), extra=()):
        waits = self._waits(eng, self._collect(reads, writes, extra))
        self.cnt[eng] += 1
        tok = ("p", eng, self.cnt[eng])
        self.q[eng].append((waits, fn, (self.sem[eng], 1)))
        self._record(tok, reads, writes)
        return tok

    def group(self, eng, fns, reads=(), writes=(), extra=()):
        waits = self._waits(eng, self._collect(reads, writes, extra))
        self.cnt[eng] += 1
        tok = ("p", eng, self.cnt[eng])
        n = len(fns)
        for i, fn in enumerate(fns):
            self.q[eng].append((waits if i == 0 else [], fn,
                               (self.sem[eng], 1) if i == n - 1 else None))
        self._record(tok, reads, writes)
        return tok

    def dma(self, eng, chan, out, in_, reads=(), writes=(), extra=(),
            chain=False):
        key = (eng, chan)
        if key not in self.dsem:
            self.dsem[key] = self.nc.alloc_semaphore(f"dma_{eng}_{chan}")
            self.dcnt[key] = 0
        if chain and self.dcnt[key] > 0:
            extra = list(extra) + [("d", key, self.dcnt[key])]
        waits = self._waits(eng, self._collect(reads, writes, extra))
        self.dcnt[key] += 1
        tok = ("d", key, self.dcnt[key])

        def fn(e, out=out, in_=in_):
            return e.dma_start(out=out, in_=in_)

        self.q[eng].append((waits, fn, (self.dsem[key], 16)))
        self._record(tok, reads, writes)
        return tok

    def dma_total(self, eng, chan):
        key = (eng, chan)
        return ("d", key, self.dcnt.get(key, 0))

    def replay(self):
        nc = self.nc
        with nc.Block() as block:
            for ename in self.ENGS:
                ops = self.q[ename]

                def body(eng, ops=ops):
                    for waits, fn, inc in ops:
                        for sem, val in waits:
                            eng.wait_ge(sem, val)
                        inst = fn(eng)
                        if inc is not None:
                            inst.then_inc(*inc)

                getattr(block, ename)(body)




def build(nch=32):
    T = nch * 128
    nc = bass.Bass("TRN2", target_bir_lowering=False)

    x_hbm = nc.dram_tensor("x", [T, D], F32, kind="ExternalInput").ap()
    out_hbm = nc.dram_tensor("out", [T, D], F32, kind="ExternalOutput").ap()
    guid_hbm = nc.dram_tensor("guid", [G, D], F32, kind="ExternalInput").ap()
    wup_hbm = nc.dram_tensor("wup", [F, D], F32, kind="ExternalInput").ap()
    wdn_hbm = nc.dram_tensor("wdn", [D, F], F32, kind="ExternalInput").ap()

    ARENA = 212544
    arena_base, _ = nc.bump_sbuf(ARENA)
    off = [0]
    offs = {}

    def sb(name, shape, dtype, align=32, at=None):
        nbytes = int(np.prod(shape[1:])) * mybir.dt.size(dtype)
        if at is None:
            off[0] = (off[0] + align - 1) // align * align
            at = off[0]
            off[0] += nbytes
        offs[name] = at
        return nc.alloc_sbuf_tensor_at(name, list(shape), dtype,
                                       offset=arena_base + at).ap()

    wqupT = sb("wqupT", [128, 8 * F], BF16)
    wqdnT = sb("wqdnT", [128, 32 * D], BF16)
    gsT = sb("gsT", [128, 8 * G], F32)
    guidN = sb("guidN", [128, 4 * D], F32)
    xr = [sb(f"xr{i}", [128, D], F32) for i in range(3)]
    BIG = sb("BIGf", [128, 4096], F32)
    gfm_h = sb("gfm_h", [128, 1024], BF16, at=offs["BIGf"] + 8192)
    gfm_l = sb("gfm_l", [128, 1024], BF16, at=offs["BIGf"] + 10240)
    xnT_h = sb("xnT_h", [128, 1024], BF16)
    xnT_l = sb("xnT_l", [128, 1024], BF16)
    SEC_a = sb("SEC_a", [128, 1024], BF16)
    SEC_b = sb("SEC_b", [128, 1024], BF16)
    xr.append(sb("xr3", [128, D], F32, at=offs["SEC_a"]))
    SECFULL = sb("SECFULL", [128, 4096], BF16, at=offs["xnT_h"])
    astgU = sb("astgU", [128, 16384], F32, at=offs["wqupT"])
    astgD = sb("astgD", [128, 16384], F32, at=offs["wqdnT"])
    p32 = sb("p32", [128, G], F32, at=offs["BIGf"] + 8192)
    pT = sb("pT", [128, G], F32, at=offs["BIGf"] + 10240)
    er = [sb(f"er{i}", [128, D], F32) for i in range(2)]
    bandC = sb("bandC", [128, 128], F32)
    bandP = sb("bandP", [128, 128], F32)
    bandC0 = sb("bandC0", [128, 128], F32)
    identf = sb("identf", [128, 128], F32)
    mshift = sb("mshift", [128, 128], F32)
    identb = sb("identb", [128, 128], BF16)
    ln2c1r = [sb(f"ln2c1_{i}", [128, 8], F32) for i in range(4)]
    onesr = sb("onesr", [1, 128], F32, at=offs["er0"])
    magrow = sb("magrow", [1, 128], F32, at=offs["SEC_a"])
    j8 = sb("j8", [128, 8], F32, at=offs["SEC_a"])
    s8 = sb("s8", [128, 8], F32, at=offs["SEC_a"] + 32)
    onesf = sb("onesf", [128, 1], F32, align=32)
    ones1 = sb("ones1", [1, 1], F32, align=32)
    t2r = [sb(f"t2r{i}", [128, 1], F32, align=32) for i in range(2)]
    _colnames = ("sxc", "sqc", "mcol", "vcol", "acol", "bcol", "scol",
                 "sg0", "sg1", "sg2", "sg3", "sgc",
                 "magsq", "magc", "imc", "maskc", "anc", "rc", "alc", "bec",
                 "mxc", "nbc", "sumc", "invw", "tmp0", "tmp1", "tmp2",
                 "thrp", "wacc", "a2r0", "a2r1", "a2r2", "a2r3",
                 "b2r0", "b2r1", "b2r2", "b2r3", "eps5",
                 "hs0_0", "hs1_0", "hs0_1", "hs1_1")
    colpack = sb("colpack", [128, len(_colnames)], F32)
    cols = {n: colpack[:, i:i + 1] for i, n in enumerate(_colnames)}
    assert off[0] <= ARENA, f"SBUF over budget: {off[0]}"

    BS = [BIG[:, i * 1024:(i + 1) * 1024] for i in range(4)]

    PT_A = nc.alloc_psum_tensor("PT_A", [128, 1024], F32).ap()
    PT_B = nc.alloc_psum_tensor("PT_B", [128, 1024], F32).ap()
    PT_C = nc.alloc_psum_tensor("PT_C", [128, 512], F32).ap()
    PT_E = nc.alloc_psum_tensor("PT_E", [128, 512], F32).ap()
    PT_F = nc.alloc_psum_tensor("PT_F", [128, 512], F32).ap()
    PT_G = nc.alloc_psum_tensor("PT_G", [128, 512], F32).ap()

    em = Em(nc)

    bF = {}
    for name in ("wqupT", "wqdnT", "gsT", "guidN", "S0", "S1", "S2", "S3",
                 "xnT_h", "xnT_l", "SEC_a", "SEC_b", "band", "ident", "ones",
                 "ln2c1_0", "ln2c1_1", "ln2c1_2", "ln2c1_3", "t2r0", "t2r1", "msh",
                 "PT_A", "PT_Blo", "PT_Bhi", "PT_C", "PT_E", "PT_F", "PT_G",
                 "gfm_h", "gfm_l"):
        bF[name] = Buf()
    bX = [Buf() for _ in range(4)]
    bE = [Buf() for _ in range(2)]
    bC = {name: Buf() for name in cols}
    SECALL = (bF["xnT_h"], bF["xnT_l"], bF["SEC_a"], bF["SEC_b"])
    BIGALL = (bF["S0"], bF["S1"], bF["S2"], bF["S3"])

    def vop(fn, reads=(), writes=(), extra=()):
        return em.op("vector", fn, reads, writes, extra)

    def sop(fn, reads=(), writes=(), extra=()):
        return em.op("scalar", fn, reads, writes, extra)

    def gop(fn, reads=(), writes=(), extra=()):
        return em.op("gpsimd", fn, reads, writes, extra)

    def top(fn, reads=(), writes=(), extra=()):
        return em.op("tensor", fn, reads, writes, extra)

    def tgroup(fns, reads=(), writes=(), extra=()):
        return em.group("tensor", fns, reads, writes, extra)

    def sqrt_col(dst, srcname, bias=0.0, scale_after=0.5):
        if bias == 0.0:
            sop(lambda e: e.activation(cols[dst], cols[srcname], AF.Ln),
                reads=[bC[srcname]], writes=[bC[dst]])
        else:
            sop(lambda e: e.activation(cols[dst], cols[srcname], AF.Ln,
                                       bias=cols["eps5"]),
                reads=[bC[srcname], bC["eps5"]], writes=[bC[dst]])
        sop(lambda e: e.activation(cols[dst], cols[dst], AF.Exp,
                                   scale=scale_after),
            writes=[bC[dst]])


    def make_ident(t, bufkey):
        gop(lambda e, t=t: e.memset(t, 0.0), writes=[bF[bufkey]])
        gop(lambda e, t=t: e.affine_select(
            out=t, in_=t, pattern=[[-1, 128]], compare_op=ALU.not_equal,
            fill=1.0, base=0, channel_multiplier=1), writes=[bF[bufkey]])

    make_ident(identf, "ident")
    gop(lambda e: e.memset(mshift, 0.0), writes=[bF["msh"]])
    gop(lambda e: e.affine_select(
        out=mshift, in_=mshift, pattern=[[-1, 128]], compare_op=ALU.not_equal,
        fill=-1.0, base=0, channel_multiplier=1), writes=[bF["msh"]])
    gop(lambda e: e.affine_select(
        out=mshift, in_=mshift, pattern=[[-1, 128]], compare_op=ALU.not_equal,
        fill=1.0, base=-1, channel_multiplier=1), writes=[bF["msh"]])
    make_ident(identb, "ident")
    gop(lambda e: e.memset(onesf, 1.0), writes=[bF["ones"]])
    gop(lambda e: e.memset(ones1, 1.0), writes=[bF["ones"]])

    def band_tile(t, base_k, mask2_base, mask2_mult):
        gop(lambda e, t=t, b=base_k: e.iota(
            t, pattern=[[-1, 128]], base=b, channel_multiplier=1,
            allow_small_or_imprecise_dtypes=True), writes=[bF["band"]])
        vop(lambda e, t=t: e.tensor_scalar_mul(t, t, 0.9), writes=[bF["band"]])
        vop(lambda e, t=t: e.tensor_scalar_mul(t, t, 1.0 / 7.0), writes=[bF["band"]])
        vop(lambda e, t=t: e.tensor_scalar_add(t, t, 0.1), writes=[bF["band"]])
        gop(lambda e, t=t, b=base_k: e.affine_select(
            out=t, in_=t, pattern=[[-1, 128]], compare_op=ALU.is_ge,
            fill=0.0, base=b, channel_multiplier=1), writes=[bF["band"]])
        gop(lambda e, t=t, b=mask2_base, m=mask2_mult: e.affine_select(
            out=t, in_=t, pattern=[[1, 128]], compare_op=ALU.is_ge,
            fill=0.0, base=b, channel_multiplier=m), writes=[bF["band"]])

    band_tile(bandC, 8, -1, -1)
    band_tile(bandP, -120, 127, -1)
    vop(lambda e: e.tensor_copy(bandC0, bandC), reads=[bF["band"]], writes=[bF["band"]])
    gop(lambda e: e.iota(j8, pattern=[[0, 8]], base=0, channel_multiplier=1,
                         allow_small_or_imprecise_dtypes=True), writes=[bF["SEC_a"]])
    gop(lambda e: e.iota(s8, pattern=[[1, 8]], base=0, channel_multiplier=0,
                         allow_small_or_imprecise_dtypes=True), writes=[bF["SEC_a"]])
    vop(lambda e: e.tensor_scalar_sub(s8, s8, 1.0), writes=[bF["SEC_a"]])
    vop(lambda e: e.tensor_scalar_max(s8, s8, 1.0), writes=[bF["SEC_a"]])
    vop(lambda e: e.reciprocal(s8, s8), writes=[bF["SEC_a"]])
    vop(lambda e: e.tensor_scalar_mul(j8, j8, 0.9), writes=[bF["SEC_a"]])
    vop(lambda e: e.tensor_mul(j8, j8, s8), reads=[bF["SEC_a"]], writes=[bF["SEC_a"]])
    vop(lambda e: e.tensor_scalar_add(j8, j8, 0.1), writes=[bF["SEC_a"]])
    gop(lambda e: e.affine_select(out=j8, in_=j8, pattern=[[1, 8]],
                                  compare_op=ALU.is_ge, fill=0.0, base=-1,
                                  channel_multiplier=-1), writes=[bF["SEC_a"]])
    gop(lambda e: e.affine_select(out=j8, in_=j8, pattern=[[1, 8]],
                                  compare_op=ALU.is_ge, fill=0.0, base=-1,
                                  channel_multiplier=0), writes=[bF["SEC_a"]])
    vop(lambda e: e.tensor_copy(bandC0[:, 0:8], j8), reads=[bF["SEC_a"]], writes=[bF["band"]])

    gop(lambda e: e.memset(cols["eps5"], 1e-5), writes=[bC["eps5"]])

    _loaded = set()

    def load_x(c):
        if c < nch and c not in _loaded:
            _loaded.add(c)
            extra = []
            if c == 3:
                extra = [("p", "vector", em.cnt["vector"]),
                         ("p", "scalar", em.cnt["scalar"]),
                         ("p", "tensor", em.cnt["tensor"])]
            em.dma("sync", f"x{c % 4}", xr[c % 4],
                   x_hbm[c * 128:(c + 1) * 128, :],
                   writes=[bX[c % 4]], extra=extra)

    load_x(0)
    load_x(1)
    load_x(2)

    _gtoks = []
    for gc in range(4):
        for k in range(4):
            _gtoks.append(em.dma(
                "sync", f"wa{k}",
                guidN[:, gc * D + k * 256: gc * D + (k + 1) * 256],
                guid_hbm[gc * 128:(gc + 1) * 128, k * 256:(k + 1) * 256],
                chain=True))
    bF["guidN"].wtok = _gtoks
    bF["guidN"].rtoks = []
    for gc in range(4):
        gsl = guidN[:, gc * D:(gc + 1) * D]
        sop(lambda e, gsl=gsl: e.activation(BS[1], gsl, AF.Square,
                                            accum_out=cols["tmp0"]),
            reads=[bF["guidN"]], writes=[bF["S1"], bC["tmp0"]])
        vop(lambda e: e.tensor_scalar_max(cols["tmp0"], cols["tmp0"], 1e-30),
            writes=[bC["tmp0"]])
        sqrt_col("tmp0", "tmp0")
        vop(lambda e: e.tensor_scalar_max(cols["tmp0"], cols["tmp0"], 1e-8),
            writes=[bC["tmp0"]])
        vop(lambda e: e.reciprocal(cols["tmp1"], cols["tmp0"]),
            reads=[bC["tmp0"]], writes=[bC["tmp1"]])
        sop(lambda e, gsl=gsl: e.activation(BS[0], gsl, AF.Identity,
                                            scale=cols["tmp1"]),
            reads=[bF["guidN"], bC["tmp1"]], writes=[bF["S0"]])
        tgroup([
            (lambda e, dj=dj: e.transpose(PT_B[:, dj * 128:(dj + 1) * 128],
                                          BS[0][:, dj * 128:(dj + 1) * 128],
                                          identf))
            for dj in range(8)
        ], reads=[bF["S0"], bF["ident"]], writes=[bF["PT_Blo"], bF["PT_Bhi"]])
        em.group("scalar", [
            (lambda e, dj=dj, gc=gc: e.copy(
                gsT[:, dj * G + gc * 128: dj * G + (gc + 1) * 128],
                PT_B[:, dj * 128:(dj + 1) * 128]))
            for dj in range(8)
        ], reads=[bF["PT_Blo"], bF["PT_Bhi"]], writes=[bF["gsT"]])

    gop(lambda e: e.memset(onesr, 1.0), writes=[bE[0]])

    BIGH = [BIG[:, 0:2048], BIG[:, 2048:4096]]
    BIGH3 = [BIG.rearrange("p (j d) -> p j d", j=4)[:, 0:2, :],
             BIG.rearrange("p (j d) -> p j d", j=4)[:, 2:4, :]]
    SECH = [SECFULL[:, 0:2048], SECFULL[:, 2048:4096]]
    BHB = [(bF["S0"], bF["S1"]), (bF["S2"], bF["S3"])]
    SHB = [(bF["xnT_h"], bF["xnT_l"]), (bF["SEC_a"], bF["SEC_b"])]

    def wup_half_ap(p):
        return wup_hbm[p * 256:(p + 1) * 256, :].rearrange("(j p) d -> p j d", p=128)

    def wdn_half_ap(i):
        p, hh = i // 2, i % 2
        return wdn_hbm[p * 128:(p + 1) * 128, hh * 2048:(hh + 1) * 2048]

    def alpha_memset():
        vop(lambda e: e.memset(cols["wacc"], 0.0), writes=[bC["wacc"]])

    def alpha_piece(i, is3d, ap, astg, slots, use_s=False):
        sl = i % 8
        dst2 = astg[:, sl * 2048:(sl + 1) * 2048]
        prior = list(slots[sl].rtoks) + [slots[sl].wtok]
        toks = []
        for k in range(4):
            if is3d:
                dst3 = dst2.rearrange("p (j d) -> p j d", j=2)
                d = dst3[:, k // 2:k // 2 + 1, (k % 2) * 512:(k % 2 + 1) * 512]
                s = ap[:, k // 2:k // 2 + 1, (k % 2) * 512:(k % 2 + 1) * 512]
            else:
                d = dst2[:, k * 512:(k + 1) * 512]
                s = ap[:, k * 512:(k + 1) * 512]
            toks.append(em.dma("sync", f"wa{k}", d, s, extra=prior, chain=True))
        slots[sl].wtok = toks
        slots[sl].rtoks = []
        tc = cols["tmp1"] if i % 2 else cols["tmp0"]
        tb = bC["tmp1"] if i % 2 else bC["tmp0"]
        if use_s:
            sop(lambda e, dst2=dst2, tc=tc: e.activation(
                dst2, dst2, AF.Abs, accum_out=tc),
                reads=[slots[sl]], writes=[slots[sl], tb])
        else:
            vop(lambda e, dst2=dst2, tc=tc: e.tensor_reduce(
                tc, dst2, AX.X, ALU.add, apply_absolute_value=True),
                reads=[slots[sl]], writes=[tb])
        vop(lambda e, tc=tc: e.tensor_add(cols["wacc"], cols["wacc"], tc),
            reads=[tb], writes=[bC["wacc"]])

    def alpha_thr():
        top(lambda e: e.matmul(PT_C[0:1, 0:1], cols["wacc"], onesf,
                               start=True, stop=True),
            reads=[bC["wacc"], bF["ones"]], writes=[bF["PT_C"]])
        vop(lambda e: e.tensor_copy(cols["tmp0"][0:1, :], PT_C[0:1, 0:1]),
            reads=[bF["PT_C"]], writes=[bC["tmp0"]])
        vop(lambda e: e.tensor_scalar_mul(cols["tmp0"][0:1, :],
                                          cols["tmp0"][0:1, :],
                                          0.5 / float(F * D)), writes=[bC["tmp0"]])
        top(lambda e: e.matmul(PT_C[:, 300:301], onesr, cols["tmp0"][0:1, :],
                               start=True, stop=True),
            reads=[bE[0], bC["tmp0"]], writes=[bF["PT_C"]])
        vop(lambda e: e.tensor_copy(cols["thrp"], PT_C[:, 300:301]),
            reads=[bF["PT_C"]], writes=[bC["thrp"]])

    SECQ = [SECFULL[:, q * 1024:(q + 1) * 1024] for q in range(4)]
    SQB = [bF["xnT_h"], bF["xnT_l"], bF["SEC_a"], bF["SEC_b"]]
    BSB = [bF["S0"], bF["S1"], bF["S2"], bF["S3"]]

    def tern_dma_q(p, src_ap):
        s = p % 4
        prior = list(BSB[s].rtoks) + [BSB[s].wtok]
        toks = []
        for k in range(4):
            toks.append(em.dma("sync", f"wc{k}",
                               BS[s][:, k * 256:(k + 1) * 256],
                               src_ap[:, k * 256:(k + 1) * 256],
                               extra=prior, chain=True))
        BSB[s].wtok = toks
        BSB[s].rtoks = []

    def sign_abs_q(p):
        s = p % 4
        sop(lambda e, s=s: e.activation(SECQ[s], BS[s], AF.Sign),
            reads=[BSB[s]], writes=[SQB[s]])
        sop(lambda e, s=s: e.activation(BS[s], BS[s], AF.Abs),
            writes=[BSB[s]])

    def isgt_mul_q(p):
        s = p % 4
        vop(lambda e, s=s: e.tensor_scalar(BS[s], BS[s], cols["thrp"],
                                           None, ALU.is_gt),
            reads=[bC["thrp"]], writes=[BSB[s]])
        vop(lambda e, s=s: e.tensor_mul(SECQ[s], SECQ[s], BS[s]),
            reads=[BSB[s]], writes=[SQB[s]])

    def wq_tr_evac_q(p, dsts, wbuf, aslots):
        s = p % 4
        pts = ((PT_F, bF["PT_F"]), (PT_G, bF["PT_G"])) if p % 2 == 0 else \
              ((PT_C, bF["PT_C"]), (PT_E, bF["PT_E"]))
        tgroup([
            (lambda e, s=s, dj=dj: e.matmul(
                pts[dj // 4][0][:, (dj % 4) * 128:(dj % 4 + 1) * 128],
                SECQ[s][:, dj * 128:(dj + 1) * 128],
                identb, start=True, stop=True))
            for dj in range(8)
        ], reads=[SQB[s], bF["ident"]], writes=[pts[0][1], pts[1][1]])
        eng = "vector" if p % 2 == 0 else "scalar"
        em.group(eng, [
            (lambda e, dj=dj, d=d: (e.tensor_copy(d, pts[dj // 4][0][:, (dj % 4) * 128:(dj % 4 + 1) * 128])
                                    if eng == "vector" else
                                    e.copy(d, pts[dj // 4][0][:, (dj % 4) * 128:(dj % 4 + 1) * 128])))
            for dj, d in enumerate(dsts)
        ], reads=[pts[0][1], pts[1][1]], writes=[wbuf] + aslots)

    upslots = [Buf() for _ in range(8)]
    dnslots = [Buf() for _ in range(8)]
    alpha_memset()
    for p in range(16):
        alpha_piece(p, True, wup_half_ap(p), astgU, upslots, use_s=(p % 2 == 1))
    alpha_thr()
    alpha_memset()
    tern_dma_q(0, wup_hbm[0:128, :])
    tern_dma_q(1, wup_hbm[128:256, :])
    sign_abs_q(0)
    for p in range(32):
        if p + 2 < 32:
            tern_dma_q(p + 2, wup_hbm[(p + 2) * 128:(p + 3) * 128, :])
        if p + 1 < 32:
            sign_abs_q(p + 1)
        if p % 2 == 0:
            alpha_piece(p // 2, False, wdn_half_ap(p // 2), astgD, dnslots)
        isgt_mul_q(p)
        wq_tr_evac_q(p,
                     [wqupT[:, dj * F + p * 128: dj * F + (p + 1) * 128]
                      for dj in range(8)],
                     wbuf=bF["wqupT"], aslots=upslots)
    alpha_thr()
    def wdn_q_ap(i):
        return wdn_hbm[(i // 4) * 128:(i // 4 + 1) * 128,
                       (i % 4) * 1024:(i % 4 + 1) * 1024]

    tern_dma_q(0, wdn_q_ap(0))
    tern_dma_q(1, wdn_q_ap(1))
    sign_abs_q(0)
    for i in range(32):
        if i + 2 < 32:
            tern_dma_q(i + 2, wdn_q_ap(i + 2))
        if i + 1 < 32:
            sign_abs_q(i + 1)
        isgt_mul_q(i)
        r, q = i // 4, i % 4
        wq_tr_evac_q(i,
                     [wqdnT[:, (q * 8 + dj) * D + r * 128: (q * 8 + dj) * D + (r + 1) * 128]
                      for dj in range(8)],
                     wbuf=bF["wqdnT"], aslots=dnslots)

    for _g in ("gfm_h", "gfm_l"):
        bF[_g].wtok = bF["S2"].wtok
        bF[_g].rtoks = list(bF["S2"].rtoks)

    def ln_stats(xa, xbuf, fused=False, par=0):
        if fused:
            h0, h1 = f"hs0_{par}", f"hs1_{par}"
            vop(lambda e, h0=h0, h1=h1: e.tensor_add(cols["sxc"], cols[h0],
                                                     cols[h1]),
                reads=[bC[h0], bC[h1]], writes=[bC["sxc"]])
        else:
            vop(lambda e, xa=xa: e.reduce_sum(cols["sxc"], xa, AX.X),
                reads=[xbuf], writes=[bC["sxc"]])
        sop(lambda e, xa=xa: e.activation(BS[2], xa, AF.Square,
                                          accum_out=cols["sqc"]),
            reads=[xbuf], writes=[bF["S2"], bF["gfm_h"], bF["gfm_l"],
                                  bC["sqc"]])
        vop(lambda e: e.tensor_scalar_mul(cols["mcol"], cols["sxc"], 1.0 / D),
            reads=[bC["sxc"]], writes=[bC["mcol"]])
        vop(lambda e: e.tensor_mul(cols["tmp0"], cols["mcol"], cols["mcol"]),
            reads=[bC["mcol"]], writes=[bC["tmp0"]])
        vop(lambda e: e.tensor_scalar(cols["vcol"], cols["sqc"], 1.0 / D,
                                      cols["tmp0"], ALU.mult, ALU.subtract),
            reads=[bC["sqc"], bC["tmp0"]], writes=[bC["vcol"]])

    def ln_affine_ffn():
        vop(lambda e: e.tensor_scalar_max(cols["tmp1"], cols["vcol"], 1e-30),
            reads=[bC["vcol"]], writes=[bC["tmp1"]])
        sqrt_col("tmp1", "tmp1")
        sop(lambda e: e.activation(cols["tmp2"], cols["vcol"], AF.Ln,
                                   bias=cols["eps5"]),
            reads=[bC["vcol"], bC["eps5"]], writes=[bC["tmp2"]])
        sop(lambda e: e.activation(cols["tmp2"], cols["tmp2"], AF.Exp,
                                   scale=0.5), writes=[bC["tmp2"]])
        vop(lambda e: e.scalar_tensor_tensor(cols["tmp0"], cols["tmp2"], 1e-8,
                                             cols["tmp1"], ALU.mult, ALU.add),
            reads=[bC["tmp1"], bC["tmp2"]], writes=[bC["tmp0"]])
        vop(lambda e: e.reciprocal(cols["acol"], cols["tmp0"]),
            reads=[bC["tmp0"]], writes=[bC["acol"]])
        vop(lambda e: e.scalar_tensor_tensor(cols["bcol"], cols["mcol"], -1.0,
                                             cols["acol"], ALU.mult, ALU.mult),
            reads=[bC["mcol"], bC["acol"]], writes=[bC["bcol"]])

    def ln_affine_plain(da, db):
        sop(lambda e: e.activation(cols["tmp0"], cols["vcol"], AF.Ln,
                                   bias=cols["eps5"]),
            reads=[bC["vcol"], bC["eps5"]], writes=[bC["tmp0"]])
        sop(lambda e: e.activation(cols["tmp0"], cols["tmp0"], AF.Exp,
                                   scale=0.5), writes=[bC["tmp0"]])
        vop(lambda e, da=da: e.reciprocal(cols[da], cols["tmp0"]),
            reads=[bC["tmp0"]], writes=[bC[da]])
        vop(lambda e, da=da, db=db: e.scalar_tensor_tensor(
            cols[db], cols["mcol"], -1.0, cols[da], ALU.mult, ALU.mult),
            reads=[bC["mcol"], bC[da]], writes=[bC[db]])

    def ffn_sc(c, fused=False):
        xa = xr[c % 4]
        xbuf = bX[c % 4]
        ln_stats(xa, xbuf, fused=fused, par=c % 2)
        ln_affine_ffn()
        sop(lambda e, xa=xa: e.activation(BS[1], xa, AF.Identity,
                                          scale=cols["acol"], bias=cols["bcol"]),
            reads=[xbuf, bC["acol"], bC["bcol"]], writes=[bF["S1"]])

    def trsplit(src, srcbuf, dst_h, dst_l, dstbufs, cast_s=False):
        pbs = (bF["PT_Blo"], bF["PT_Bhi"])
        for half in range(2):
            tgroup([
                (lambda e, src=src, dj=dj: e.transpose(
                    PT_B[:, dj * 128:(dj + 1) * 128],
                    src[:, dj * 128:(dj + 1) * 128], identf))
                for dj in range(half * 4, half * 4 + 4)
            ], reads=[srcbuf, bF["ident"]], writes=[pbs[half]])
        for half in range(2):
            if cast_s:
                sop(lambda e, dst_h=dst_h, half=half: e.copy(
                    dst_h[:, half * 512:(half + 1) * 512],
                    PT_B[:, half * 512:(half + 1) * 512]),
                    reads=[pbs[half]], writes=[dstbufs[0]])
            else:
                vop(lambda e, dst_h=dst_h, half=half: e.tensor_copy(
                    dst_h[:, half * 512:(half + 1) * 512],
                    PT_B[:, half * 512:(half + 1) * 512]),
                    reads=[pbs[half]], writes=[dstbufs[0]])
        for half in range(2):
            vop(lambda e, dst_h=dst_h, dst_l=dst_l, half=half: e.tensor_sub(
                dst_l[:, half * 512:(half + 1) * 512],
                PT_B[:, half * 512:(half + 1) * 512],
                dst_h[:, half * 512:(half + 1) * 512]),
                reads=[pbs[half], dstbufs[0]], writes=[dstbufs[1]])

    def ffn_tr(c):
        trsplit(BS[1], bF["S1"], xnT_h, xnT_l, (bF["xnT_h"], bF["xnT_l"]),
                cast_s=True)

    def ffn_pe(c):
        xa = xr[c % 4]
        xbuf = bX[c % 4]
        def gq(q):
            return (BS[0], bF["S0"]) if q % 2 == 0 else (BS[1], bF["S1"])

        def quarter_split_tr(q):
            gsrc, gbuf = gq(q)
            trsplit(gsrc, gbuf, gfm_h, gfm_l, (bF["gfm_h"], bF["gfm_l"]),
                    cast_s=True)
            vop(lambda e, gsrc=gsrc, q=q: e.scalar_tensor_tensor(
                er[0], gsrc, 1.0, gsrc, ALU.mult, ALU.mult,
                accum_out=cols[f"sg{q}"]),
                reads=[gbuf], writes=[bE[0], bC[f"sg{q}"]])

        def quarter_mm2(q):
            for pi, (src, srcb) in enumerate(((gfm_h, bF["gfm_h"]),
                                              (gfm_l, bF["gfm_l"]))):
                fns = []
                for i in range(8):
                    fg = q * 8 + i
                    for nh in range(2):
                        fns.append(lambda e, src=src, i=i, fg=fg, nh=nh, pi=pi, q=q:
                                   e.matmul(PT_A[:, nh * 512:(nh + 1) * 512],
                                            src[:, i * 128:(i + 1) * 128],
                                            wqdnT[:, fg * D + nh * 512: fg * D + (nh + 1) * 512],
                                            start=(q == 0 and pi == 0 and i == 0),
                                            stop=(q == 3 and pi == 1 and i == 7)))
                tgroup(fns, reads=[srcb, bF["wqdnT"]], writes=[bF["PT_A"]])

        mm1rot = ((PT_C, bF["PT_C"]), (PT_E, bF["PT_E"]),
                  (PT_F, bF["PT_F"]), (PT_G, bF["PT_G"]))
        for e8 in range(8):
            pt, ptb = mm1rot[e8 % 4]
            for pi, (src, srcb) in enumerate(((xnT_h, bF["xnT_h"]),
                                              (xnT_l, bF["xnT_l"]))):
                fns = []
                for dj in range(8):
                    fns.append(lambda e, src=src, dj=dj, pt=pt, e8=e8, pi=pi:
                               e.matmul(pt,
                                        src[:, dj * 128:(dj + 1) * 128],
                                        wqupT[:, dj * F + e8 * 512: dj * F + (e8 + 1) * 512],
                                        start=(pi == 0 and dj == 0),
                                        stop=(pi == 1 and dj == 7)))
                tgroup(fns, reads=[srcb, bF["wqupT"]], writes=[ptb])
            gsrc, gbuf = gq(e8 // 2)
            sop(lambda e, pt=pt, e8=e8, gsrc=gsrc: e.activation(
                gsrc[:, (e8 % 2) * 512:(e8 % 2 + 1) * 512], pt, AF.Gelu),
                reads=[ptb], writes=[gbuf])
            if e8 >= 3 and e8 % 2 == 1:
                q = (e8 - 3) // 2
                if q >= 1:
                    quarter_mm2(q - 1)
                quarter_split_tr(q)
        quarter_mm2(2)
        quarter_split_tr(3)
        quarter_mm2(3)
        vop(lambda e: e.tensor_add(cols["sgc"], cols["sg0"], cols["sg1"]),
            reads=[bC["sg0"], bC["sg1"]], writes=[bC["sgc"]])
        vop(lambda e: e.tensor_add(cols["sgc"], cols["sgc"], cols["sg2"]),
            reads=[bC["sg2"]], writes=[bC["sgc"]])
        vop(lambda e: e.tensor_add(cols["sgc"], cols["sgc"], cols["sg3"]),
            reads=[bC["sg3"]], writes=[bC["sgc"]])
        vop(lambda e: e.tensor_scalar_max(cols["tmp0"], cols["sgc"], 1e-30),
            reads=[bC["sgc"]], writes=[bC["tmp0"]])
        sqrt_col("tmp0", "tmp0")
        vop(lambda e: e.tensor_scalar(cols["tmp0"], cols["tmp0"], 1.0 / 64.0,
                                      1e-8, ALU.mult, ALU.add),
            writes=[bC["tmp0"]])
        vop(lambda e: e.reciprocal(cols["scol"], cols["tmp0"]),
            reads=[bC["tmp0"]], writes=[bC["scol"]])
        vop(lambda e: e.tensor_scalar_mul(cols["scol"], cols["scol"], 0.5),
            writes=[bC["scol"]])
        for nh in range(2):
            vop(lambda e, xa=xa, nh=nh: e.scalar_tensor_tensor(
                xa[:, nh * 512:(nh + 1) * 512], PT_A[:, nh * 512:(nh + 1) * 512],
                cols["scol"], xa[:, nh * 512:(nh + 1) * 512], ALU.mult, ALU.add,
                accum_out=cols[f"hs{nh}_{c % 2}"]),
                reads=[bF["PT_A"], bC["scol"]],
                writes=[xbuf, bC[f"hs{nh}_{c % 2}"]])

    def attn_stats(c):
        ln_stats(xr[c % 4], bX[c % 4], fused=True, par=c % 2)
        ln_affine_plain(f"a2r{c % 4}", f"b2r{c % 4}")

    def attn_apply(c, row0=False):
        xa = xr[c % 4]
        a, b = f"a2r{c % 4}", f"b2r{c % 4}"
        if row0:
            sop(lambda e, xa=xa, a=a, b=b: e.activation(
                BS[0][0:1, :], xa[0:1, :], AF.Identity,
                scale=cols[a][0:1, :], bias=cols[b][0:1, :]),
                reads=[bX[c % 4], bC[a], bC[b]], writes=[bF["S0"]])
        else:
            sop(lambda e, xa=xa, a=a, b=b: e.activation(
                BS[0], xa, AF.Identity, scale=cols[a], bias=cols[b]),
                reads=[bX[c % 4], bC[a], bC[b]], writes=[bF["S0"]])

    def attn_row0(c1):
        tgroup([
            (lambda e, dj=dj: e.matmul(PT_C[:, dj:dj + 1],
                                       BS[0][0:1, dj * 128:(dj + 1) * 128],
                                       ones1, start=True, stop=True))
            for dj in range(8)
        ], reads=[bF["S0"], bF["ones"]], writes=[bF["PT_C"]])
        vop(lambda e, c1=c1: e.tensor_copy(ln2c1r[c1 % 4], PT_C[:, 0:8]),
            reads=[bF["PT_C"]], writes=[bF[f"ln2c1_{c1 % 4}"]])

    def attn_front_pe(c):
        attn_apply(c)
        tgroup([
            (lambda e, dj=dj: e.matmul(PT_B[:, dj * 128:(dj + 1) * 128],
                                       BS[0][:, dj * 128:(dj + 1) * 128],
                                       mshift, start=True, stop=True))
            for dj in range(8)
        ], reads=[bF["S0"], bF["msh"]], writes=[bF["PT_Blo"], bF["PT_Bhi"]])
        vop(lambda e: e.tensor_copy(BS[3], PT_B),
            reads=[bF["PT_Blo"], bF["PT_Bhi"]], writes=[bF["S3"]])

    def attn_back(c):
        xa = xr[c % 4]
        xbuf = bX[c % 4]
        if c + 1 < nch:
            ln2c1 = ln2c1r[(c + 1) % 4]
            lb = bF[f"ln2c1_{(c + 1) % 4}"]
            for dj in range(8):
                vop(lambda e, dj=dj, ln2c1=ln2c1: e.tensor_add(
                    BS[3][:, dj * 128 + 127: dj * 128 + 128],
                    BS[3][:, dj * 128 + 127: dj * 128 + 128],
                    ln2c1[:, dj:dj + 1]),
                    reads=[lb], writes=[bF["S3"]])
        else:
            for dj in range(8):
                vop(lambda e, dj=dj: e.memset(
                    BS[3][:, dj * 128 + 127: dj * 128 + 128], 0.0),
                    writes=[bF["S3"]])
        sop(lambda e: e.activation(BS[2], BS[3], AF.Square),
            reads=[bF["S3"]], writes=[bF["S2"], bF["gfm_h"], bF["gfm_l"]])
        tgroup([
            (lambda e, dj=dj: e.matmul(PT_E[:, 200:201],
                                       BS[2][:, dj * 128:(dj + 1) * 128],
                                       onesf, start=(dj == 0), stop=(dj == 7)))
            for dj in range(8)
        ], reads=[bF["S2"], bF["ones"]], writes=[bF["PT_E"]])
        vop(lambda e: e.tensor_scalar_max(cols["magsq"], PT_E[:, 200:201],
                                          1e-30),
            reads=[bF["PT_E"]], writes=[bC["magsq"]])
        t2 = t2r[c % 2]
        t2b = bF[f"t2r{c % 2}"]
        sop(lambda e: e.activation(cols["magc"], cols["magsq"], AF.Ln),
            reads=[bC["magsq"]], writes=[bC["magc"]])
        sop(lambda e: e.activation(cols["magc"], cols["magc"], AF.Exp,
                                   scale=0.5), writes=[bC["magc"]])
        sop(lambda e: e.activation(cols["tmp2"], cols["magc"], AF.Exp,
                                   scale=-4.0),
            reads=[bC["magc"]], writes=[bC["tmp2"]])
        vop(lambda e: e.tensor_scalar_add(cols["tmp1"], cols["tmp2"], 1.0),
            reads=[bC["tmp2"]], writes=[bC["tmp1"]])
        vop(lambda e: e.reciprocal(cols["tmp1"], cols["tmp1"]),
            writes=[bC["tmp1"]])
        vop(lambda e: e.tensor_scalar(cols["tmp2"], cols["tmp2"], -1.0, 1.0,
                                      ALU.mult, ALU.add), writes=[bC["tmp2"]])
        vop(lambda e, t2=t2: e.tensor_mul(t2, cols["tmp2"], cols["tmp1"]),
            reads=[bC["tmp1"], bC["tmp2"]], writes=[t2b])
        vop(lambda e: e.tensor_scalar_max(cols["tmp0"], cols["magc"], 1e-30),
            reads=[bC["magc"]], writes=[bC["tmp0"]])
        vop(lambda e: e.reciprocal(cols["imc"], cols["tmp0"]),
            reads=[bC["tmp0"]], writes=[bC["imc"]])
        vop(lambda e: e.tensor_scalar(cols["maskc"], cols["magc"], 1e-6, None,
                                      ALU.is_gt),
            reads=[bC["magc"]], writes=[bC["maskc"]])
        vop(lambda e: e.tensor_mul(cols["anc"], cols["magc"], cols["imc"]),
            reads=[bC["magc"], bC["imc"]], writes=[bC["anc"]])
        vop(lambda e: e.tensor_scalar_max(cols["anc"], cols["anc"], 1e-8),
            writes=[bC["anc"]])
        vop(lambda e: e.reciprocal(cols["tmp1"], cols["anc"]),
            reads=[bC["anc"]], writes=[bC["tmp1"]])
        vop(lambda e: e.tensor_mul(cols["rc"], cols["imc"], cols["tmp1"]),
            reads=[bC["imc"], bC["tmp1"]], writes=[bC["rc"]])
        vop(lambda e: e.tensor_scalar(cols["rc"], cols["rc"], 2.0,
                                      cols["maskc"], ALU.mult, ALU.mult),
            reads=[bC["maskc"]], writes=[bC["rc"]])
        vop(lambda e, t2=t2: e.tensor_mul(cols["alc"], t2, cols["imc"]),
            reads=[t2b, bC["imc"]], writes=[bC["alc"]])
        vop(lambda e: e.tensor_scalar(cols["alc"], cols["alc"], 0.6,
                                      cols["maskc"], ALU.mult, ALU.mult),
            reads=[bC["maskc"]], writes=[bC["alc"]])
        tgroup([
            (lambda e, dj=dj: e.matmul(PT_C, BS[3][:, dj * 128:(dj + 1) * 128],
                                       gsT[:, dj * G:(dj + 1) * G],
                                       start=(dj == 0), stop=(dj == 7)))
            for dj in range(8)
        ], reads=[bF["S3"], bF["gsT"]], writes=[bF["PT_C"]])
        tgroup([
            (lambda e, dj=dj: e.transpose(PT_A[:, dj * 128:(dj + 1) * 128],
                                          BS[3][:, dj * 128:(dj + 1) * 128],
                                          identf))
            for dj in range(8)
        ], reads=[bF["S3"], bF["ident"]], writes=[bF["PT_A"]])
        vop(lambda e: e.reduce_max(cols["mxc"], PT_C, AX.X),
            reads=[bF["PT_C"]], writes=[bC["mxc"]])
        vop(lambda e: e.scalar_tensor_tensor(cols["nbc"], cols["rc"], -1.0,
                                             cols["mxc"], ALU.mult, ALU.mult),
            reads=[bC["rc"], bC["mxc"]], writes=[bC["nbc"]])
        sop(lambda e: e.activation(p32, PT_C, AF.Exp, scale=cols["rc"],
                                   bias=cols["nbc"], accum_out=cols["sumc"]),
            reads=[bF["PT_C"], bC["rc"], bC["nbc"]],
            writes=[bF["S2"], bF["gfm_h"], bC["sumc"]])
        vop(lambda e: e.reciprocal(cols["tmp0"], cols["sumc"]),
            reads=[bC["sumc"]], writes=[bC["tmp0"]])
        vop(lambda e, t2=t2: e.scalar_tensor_tensor(cols["bec"], t2, 0.4,
                                                    cols["tmp0"], ALU.mult,
                                                    ALU.mult),
            reads=[t2b, bC["tmp0"]], writes=[bC["bec"]])
        tgroup([
            (lambda e, gc=gc: e.transpose(PT_E[:, gc * 128:(gc + 1) * 128],
                                          p32[:, gc * 128:(gc + 1) * 128],
                                          identf))
            for gc in range(4)
        ], reads=[bF["gfm_h"], bF["ident"]], writes=[bF["PT_E"]])
        sop(lambda e: e.copy(pT, PT_E), reads=[bF["PT_E"]],
            writes=[bF["S2"], bF["gfm_l"]])
        fns = []
        for gc in range(4):
            for nh in range(2):
                fns.append(lambda e, gc=gc, nh=nh: e.matmul(
                    PT_B[:, nh * 512:(nh + 1) * 512],
                    pT[:, gc * 128:(gc + 1) * 128],
                    guidN[:, gc * D + nh * 512: gc * D + (nh + 1) * 512],
                    start=(gc == 0), stop=(gc == 3)))
        tgroup(fns, reads=[bF["gfm_l"], bF["guidN"]], writes=[bF["PT_Blo"], bF["PT_Bhi"]])
        ec = er[c % 2]
        ecb = bE[c % 2]
        sop(lambda e, ec=ec: e.activation(ec, PT_A, AF.Identity,
                                          scale=cols["alc"]),
            reads=[bF["PT_A"], bC["alc"]], writes=[ecb])
        vop(lambda e, ec=ec: e.scalar_tensor_tensor(ec, PT_B, cols["bec"], ec,
                                                    ALU.mult, ALU.add),
            reads=[bF["PT_Blo"], bF["PT_Bhi"], bC["bec"], ecb], writes=[ecb])
        ep = er[(c - 1) % 2]
        epb = bE[(c - 1) % 2]
        t2p = t2r[(c - 1) % 2]
        t2pb = bF[f"t2r{(c - 1) % 2}"]
        tileC = bandC0 if c == 0 else bandC
        for pt, ptb, lo in ((PT_C, bF["PT_C"], 0), (PT_E, bF["PT_E"], 512)):
            fns = []
            if c > 0:
                fns.append(lambda e, pt=pt, lo=lo, ep=ep: e.matmul(
                    pt, bandP, ep[:, lo:lo + 512], start=True, stop=False))
            fns.append(lambda e, pt=pt, lo=lo, ec=ec, tileC=tileC: e.matmul(
                pt, tileC, ec[:, lo:lo + 512], start=(c == 0), stop=True))
            tgroup(fns, reads=[bF["band"], ecb] + ([epb] if c > 0 else []),
                   writes=[ptb])
        fns = []
        if c > 0:
            fns.append(lambda e, t2p=t2p: e.matmul(
                PT_B[:, 0:1], bandP, t2p, start=True, stop=False))
        fns.append(lambda e, t2=t2, tileC=tileC: e.matmul(
            PT_B[:, 0:1], tileC, t2, start=(c == 0), stop=True))
        tgroup(fns, reads=[bF["band"], t2b] + ([t2pb] if c > 0 else []),
               writes=[bF["PT_Blo"]])
        vop(lambda e: e.tensor_scalar_max(cols["tmp0"], PT_B[:, 0:1], 1e-8),
            reads=[bF["PT_Blo"]], writes=[bC["tmp0"]])
        vop(lambda e: e.reciprocal(cols["invw"], cols["tmp0"]),
            reads=[bC["tmp0"]], writes=[bC["invw"]])
        for nh, (pt, ptb, lo) in enumerate(((PT_C, bF["PT_C"], 0),
                                            (PT_E, bF["PT_E"], 512))):
            vop(lambda e, xa=xa, pt=pt, lo=lo, nh=nh: e.scalar_tensor_tensor(
                xa[:, lo:lo + 512], pt, cols["invw"], xa[:, lo:lo + 512],
                ALU.mult, ALU.add, accum_out=cols[f"hs{nh}_{c % 2}"]),
                reads=[ptb, bC["invw"]],
                writes=[xbuf, bC[f"hs{nh}_{c % 2}"]])

    def warm(dv=0):
        tok = ("p", "vector", max(0, em.cnt["vector"] - dv))
        em.op("tensor",
              lambda e: e.matmul(PT_C, identb, wqupT[:, 0:512],
                                 start=True, stop=True),
              reads=[bF["ident"], bF["wqupT"]], writes=[bF["PT_C"]],
              extra=[tok])

    def store(c):
        em.dma("gpsimd", f"st{c % 4}", out_hbm[c * 128:(c + 1) * 128, :],
               xr[c % 4], reads=[bX[c % 4]])

    assert nch % 2 == 0
    load_x(0)
    load_x(1)
    load_x(2)
    load_x(3)
    ffn_sc(0)
    ffn_tr(0)
    ffn_pe(0)
    attn_stats(0)
    ffn_sc(1)
    ffn_tr(1)
    ffn_pe(1)
    attn_stats(1)
    attn_apply(1, row0=True)
    attn_row0(1)
    for c in range(0, nch, 2):
        n2, n3 = c + 2 < nch, c + 3 < nch
        if n2:
            ffn_sc(c + 2)
            ffn_tr(c + 2)
        attn_front_pe(c)
        if n2:
            ffn_pe(c + 2)
            if n3:
                ffn_sc(c + 3)
            attn_stats(c + 2)
            attn_apply(c + 2, row0=True)
        if n3:
            ffn_tr(c + 3)
        if n2:
            attn_row0(c + 2)
        if n3:
            ffn_pe(c + 3)
            attn_stats(c + 3)
            attn_apply(c + 3, row0=True)
            attn_row0(c + 3)
        attn_back(c)
        attn_front_pe(c + 1)
        ffn_sc(c, fused=True)
        attn_back(c + 1)
        ffn_tr(c)
        ffn_pe(c)
        store(c)
        ffn_sc(c + 1, fused=True)
        ffn_tr(c + 1)
        ffn_pe(c + 1)
        store(c + 1)
        if n2:
            load_x(c + 4)
        if n3:
            load_x(c + 5)

    em.op("gpsimd", lambda e: e.nop(),
          extra=[em.dma_total("gpsimd", f"st{i}") for i in range(4)])

    em.replay()
    return nc


def kernel(L0_attn_embeddings, L0_attn_guidance_trajectories, L0_ffn_w_up,
           L0_ffn_w_down):
    x = np.ascontiguousarray(np.asarray(L0_attn_embeddings, dtype=np.float32))
    guid = np.ascontiguousarray(np.asarray(L0_attn_guidance_trajectories, dtype=np.float32))
    wup = np.ascontiguousarray(np.asarray(L0_ffn_w_up, dtype=np.float32))
    wdn = np.ascontiguousarray(np.asarray(L0_ffn_w_down, dtype=np.float32))
    nc = build(nch=S // 128)
    in_maps = [{"x": np.ascontiguousarray(x[b]), "guid": guid, "wup": wup,
                "wdn": wdn} for b in range(NCORE)]
    res = run_bass_kernel_spmd(nc, in_maps, core_ids=list(range(NCORE)))
    return np.stack([res.results[b]["out"] for b in range(NCORE)], axis=0)
